# version 11
# baseline (speedup 1.0000x reference)
import math
from contextlib import ExitStack

import numpy as np
import ml_dtypes

import concourse.bass as bass
import concourse.mybir as mybir
from concourse.bass_utils import run_bass_kernel_spmd

F32 = mybir.dt.float32
BF16 = mybir.dt.bfloat16
AF = mybir.ActivationFunctionType
ALU = mybir.AluOpType
AX = mybir.AxisListType

P = 128
D = 2048
S = 4096
NH = 16
NT = 32
HID = 5461
HPAD = 5504
NHT = 43
TCH = 512
NCH = 4
EPS = 1e-5
FLAT = HID * D
MASKNEG = -60000.0
SROWS = 512

ENGS = ("sync", "tensor", "vector", "scalar", "gpsimd")


class Sched:
    def __init__(self):
        self.q = {e: [] for e in ENGS}
        self.n = {"sd": 0, "gd": 0, "g2": 0, "g3": 0, "sa": 0, "sb": 0,
                  "sc": 0, "t": 0, "v": 0, "c": 0}
        self.sems = None
        self.waited = {e: {k: 0 for k in self.n} for e in ENGS}

    def dma(self, eng, key, fn):
        self.n[key] += 1

        def run(E, fn=fn, key=key):
            fn(E).then_inc(self.sems[key], 16)

        self.q[eng].append(run)
        return (key, self.n[key])

    def op(self, eng, fn, inc=True):
        key = {"tensor": "t", "vector": "v", "scalar": "c"}[eng]
        if inc:
            self.n[key] += 1

        def run(E, fn=fn, inc=inc, key=key):
            ins = fn(E)
            if inc:
                ins.then_inc(self.sems[key], 1)

        self.q[eng].append(run)
        return (key, self.n[key])

    def barrier(self):
        snap = [(k, v) for k, v in self.n.items()]
        for eng in ENGS:
            for k, v in snap:
                self.wait(eng, (k, v))

    def waitq(self, eng, key):
        self.wait(eng, (key, self.n[key]))

    def wait(self, eng, tok):
        if tok is None:
            return
        key, cnt = tok
        if cnt <= 0 or self.waited[eng][key] >= cnt:
            return
        self.waited[eng][key] = cnt
        val = cnt * 16 if key not in ("t", "v", "c") else cnt

        def run(E, key=key, val=val):
            E.wait_ge(self.sems[key], val)

        self.q[eng].append(run)


def build_nc():
    nc = bass.Bass("TRN2", target_bir_lowering=False)

    x_ap = nc.dram_tensor("x", [S, D], F32, kind="ExternalInput").ap()
    wqkv = nc.dram_tensor("wqkv", [D, 3 * D], F32, kind="ExternalInput").ap()
    bqkvt = nc.dram_tensor("bqkvt", [P, 48], F32, kind="ExternalInput").ap()
    wproj = nc.dram_tensor("wproj", [D, D], F32, kind="ExternalInput").ap()
    bproj = nc.dram_tensor("bproj", [1, D], F32, kind="ExternalInput").ap()
    w1 = nc.dram_tensor("w1", [HID, D], F32, kind="ExternalInput").ap()
    w3 = nc.dram_tensor("w3", [HID, D], F32, kind="ExternalInput").ap()
    w2 = nc.dram_tensor("w2", [D, HID], F32, kind="ExternalInput").ap()
    expb = nc.dram_tensor("expb", [P, 512], F32, kind="ExternalInput").ap()
    qbias = nc.dram_tensor("qbias", [NH, D], BF16, kind="ExternalInput").ap()
    mtri = nc.dram_tensor("mtri", [P, P], F32, kind="ExternalInput").ap()
    moth = nc.dram_tensor("moth", [P, P], F32, kind="ExternalInput").ap()
    ident = nc.dram_tensor("ident", [P, P], BF16, kind="ExternalInput").ap()
    ones1 = nc.dram_tensor("ones1", [1, P], BF16, kind="ExternalInput").ap()
    ones1f = nc.dram_tensor("ones1f", [1, P], F32, kind="ExternalInput").ap()
    onesc = nc.dram_tensor("onesc", [P, 1], BF16, kind="ExternalInput").ap()
    onescf = nc.dram_tensor("onescf", [P, 1], F32, kind="ExternalInput").ap()

    out_ap = nc.dram_tensor("out", [2048, D], F32, kind="ExternalOutput").ap()

    xn_scr = nc.dram_tensor("xn_scr", [D, S], BF16, kind="Internal").ap()
    k_scr = nc.dram_tensor("k_scr", [D, S], BF16, kind="Internal").ap()
    v_scr = nc.dram_tensor("v_scr", [D, S], BF16, kind="Internal").ap()
    q_scr = nc.dram_tensor("q_scr", [D, 2048], BF16, kind="Internal").ap()
    att_scr = nc.dram_tensor("att_scr", [D, 2048], BF16, kind="Internal").ap()
    w1t_scr = nc.dram_tensor("w1t_scr", [HPAD, D], BF16, kind="Internal").ap()
    w3t_scr = nc.dram_tensor("w3t_scr", [HPAD, D], BF16, kind="Internal").ap()
    w2t_scr = nc.dram_tensor("w2t_scr", [HPAD, D], BF16, kind="Internal").ap()
    wp_scr = nc.dram_tensor("wp_scr", [D, D], BF16, kind="Internal").ap()

    sch = Sched()
    big = ExitStack()
    en = big.enter_context
    sch.sems = {k: en(nc.semaphore("sem_" + k))
                for k in ("sd", "gd", "g2", "g3", "sa", "sb", "sc",
                          "t", "v", "c")}

    bq_sb = en(nc.sbuf_tensor("bq_sb", [P, 48], F32))
    bq_q = en(nc.sbuf_tensor("bq_q", [P, 16], F32))
    bproj_sb = en(nc.sbuf_tensor("bproj_sb", [1, D], F32))
    expb_sb = en(nc.sbuf_tensor("expb_sb", [P, 512], F32))
    mtri_sb = en(nc.sbuf_tensor("mtri_sb", [P, P], F32))
    moth_sb = en(nc.sbuf_tensor("moth_sb", [P, P], F32))
    id_sb = en(nc.sbuf_tensor("id_sb", [P, P], BF16))
    ones1_sb = en(nc.sbuf_tensor("ones1_sb", [1, P], BF16))
    ones1f_sb = en(nc.sbuf_tensor("ones1f_sb", [1, P], F32))
    onesc_sb = en(nc.sbuf_tensor("onesc_sb", [P, 1], BF16))
    onescf_sb = en(nc.sbuf_tensor("onescf_sb", [P, 1], F32))
    sacc = en(nc.sbuf_tensor("sacc", [P, 512], F32))

    sacw = en(nc.sbuf_tensor("sacw", [P, 520], F32))
    srep = en(nc.sbuf_tensor("srep", [P, 8], F32))
    srepP = en(nc.sbuf_tensor("srepP", [P, 8], F32))
    srepN = en(nc.sbuf_tensor("srepN", [P, 8], F32))
    srepM = en(nc.sbuf_tensor("srepM", [P, 2], F32))
    ssml = en(nc.sbuf_tensor("ssml", [1, 512], F32))
    sfin = en(nc.sbuf_tensor("sfin", [P, 512], F32))
    jnk = en(nc.sbuf_tensor("jnk", [P, 512], F32))


    psE_ctx = nc.psum_tensor("psE", [P, 2048], F32)
    pE = psE_ctx.__enter__()
    psT_ctx = nc.psum_tensor("psT", [P, 1024], BF16)
    pT = psT_ctx.__enter__()
    psF_ctx = nc.psum_tensor("psF", [P, 1536], F32)
    pF = psF_ctx.__enter__()
    pT3 = pT.ap().rearrange("p (m t) -> p m t", m=8)

    for dst, src in [
        (bq_sb[:], bqkvt), (bproj_sb[:], bproj), (expb_sb[:], expb),
        (mtri_sb[:], mtri), (moth_sb[:], moth),
        (id_sb[:], ident), (ones1_sb[:], ones1), (ones1f_sb[:], ones1f),
        (onesc_sb[:], onesc), (onescf_sb[:], onescf),
    ]:
        sch.dma("sync", "sd", lambda E, d=dst, s=src: E.dma_start(out=d, in_=s))
    sch.waitq("vector", "sd")
    sch.op("vector", lambda E: E.tensor_scalar_mul(bq_q[:, 0:16], bq_sb[:, 0:16],
                                                   1.0 / 128.0), inc=False)
    sch.op("vector", lambda E: E.memset(sacc[:], 0.0), inc=False)


    st01 = ExitStack()
    e01 = st01.enter_context
    xt = [e01(nc.sbuf_tensor(f"xt{i}", [P, D], F32)) for i in range(2)]
    st8 = e01(nc.sbuf_tensor("st8", [P, 16], F32))
    stA = e01(nc.sbuf_tensor("stA", [P, 2], F32))
    stB = e01(nc.sbuf_tensor("stB", [P, 2], F32))
    stM = e01(nc.sbuf_tensor("stM", [P, 2], F32))
    stE = e01(nc.sbuf_tensor("stE", [P, 2], F32))
    stM2 = e01(nc.sbuf_tensor("stM2", [P, 2], F32))
    stV = e01(nc.sbuf_tensor("stV", [P, 2], F32))
    stSq = e01(nc.sbuf_tensor("stSq", [P, 2], F32))
    stR = e01(nc.sbuf_tensor("stR", [P, 2], F32))
    stNB = e01(nc.sbuf_tensor("stNB", [P, 2], F32))
    stRn = e01(nc.sbuf_tensor("stRn", [P, 2], F32))
    epsc = e01(nc.sbuf_tensor("epsc", [P, 1], F32))
    xnt = e01(nc.sbuf_tensor("xnt", [P, D], BF16))
    xnf = e01(nc.sbuf_tensor("xnf", [P, 16 * S], BF16))
    wq32 = [e01(nc.sbuf_tensor(f"wq32_{i}", [P, D], F32)) for i in range(2)]
    sq32 = wq32[0]
    wq16 = [e01(nc.sbuf_tensor(f"wq16_{i}", [P, D], BF16)) for i in range(2)]
    asm = [e01(nc.sbuf_tensor(f"asm{i}", [P, D], BF16)) for i in range(2)]
    ws = [e01(nc.sbuf_tensor(f"ws{i}", [P, SROWS], F32)) for i in range(2)]
    xnf3 = xnf.ap().rearrange("p (db t) -> p db t", db=16)
    wqkv_r = wqkv.rearrange("(db p) c -> p db c", p=128)

    sch.waitq("tensor", "sd")
    sch.waitq("gpsimd", "sd")
    sch.waitq("scalar", "sd")
    for _ in range(16):
        sch.op("scalar", lambda E: E.activation(jnk[:], expb_sb[:], AF.Sqrt),
               inc=False)
    xt_free = [None, None]
    xnt_free = None
    pT_evac = None
    sq_free = None

    xt_tok = [None, None]

    def emit_x_dma(i):
        b = i % 2
        sch.wait("sync", xt_free[b])
        xt_tok[b] = sch.dma("sync", "sa" if b == 0 else "sb",
                            lambda E, i=i, b=b: E.dma_start(
            out=xt[b][:], in_=x_ap[128 * i:128 * (i + 1), :]))

    sch.op("vector", lambda E: E.memset(epsc[:], EPS))
    emit_x_dma(0)
    emit_x_dma(1)
    for i in range(NT):
        b = i % 2
        sch.wait("scalar", xt_tok[b])
        sch.wait("scalar", sq_free)
        sch.op("scalar", lambda E, b=b: E.activation(sq32[:], xt[b][:], AF.Square))
        tok_sq = ("c", sch.n["c"])
        sch.wait("vector", xt_tok[b])
        sch.op("vector", lambda E, b=b: E.tensor_reduce(
            stA[:, b:b + 1], xt[b][:], AX.X, ALU.add), inc=False)
        sch.wait("vector", tok_sq)
        sch.op("vector", lambda E, b=b: E.tensor_reduce(
            stB[:, b:b + 1], sq32[:], AX.X, ALU.add))
        sq_free = ("v", sch.n["v"])
        tok_red = ("v", sch.n["v"])
        sch.wait("scalar", tok_red)
        sch.op("scalar", lambda E, b=b: E.activation(
            stM[:, b:b + 1], stA[:, b:b + 1], AF.Identity,
            scale=-1.0 / D), inc=False)
        sch.op("scalar", lambda E, b=b: E.activation(
            stE[:, b:b + 1], stB[:, b:b + 1], AF.Identity,
            scale=1.0 / D, bias=epsc[:, 0:1]), inc=False)
        sch.op("scalar", lambda E, b=b: E.activation(
            stM2[:, b:b + 1], stM[:, b:b + 1], AF.Square), inc=False)
        sch.op("scalar", lambda E, b=b: E.activation(
            stV[:, b:b + 1], stM2[:, b:b + 1], AF.Identity,
            scale=-1.0, bias=stE[:, b:b + 1]), inc=False)
        sch.op("scalar", lambda E, b=b: E.activation(
            stSq[:, b:b + 1], stV[:, b:b + 1], AF.Sqrt))
        tok_sqrt = ("c", sch.n["c"])
        sch.wait("vector", tok_sqrt)
        sch.op("vector", lambda E, b=b: E.reciprocal(
            stR[:, b:b + 1], stSq[:, b:b + 1]))
        tok_r = ("v", sch.n["v"])
        sch.wait("scalar", tok_r)
        sch.op("scalar", lambda E, b=b: E.activation(
            stNB[:, b:b + 1], stM[:, b:b + 1], AF.Identity,
            scale=stR[:, b:b + 1]))
        tok_ln = ("c", sch.n["c"])
        sch.wait("scalar", tok_ln)
        sch.wait("scalar", xnt_free)
        sch.op("scalar", lambda E, b=b: E.activation(
            xnt[:], xt[b][:], AF.Identity,
            bias=stNB[:, b:b + 1], scale=stR[:, b:b + 1]))
        tok_app = ("c", sch.n["c"])
        xt_free[b] = tok_app
        if i + 2 < NT:
            emit_x_dma(i + 2)
        t_last = None
        for half in range(2):
            sch.wait("tensor", tok_app)
            sch.wait("tensor", pT_evac)
            for m in range(8):
                db = 8 * half + m
                sch.op("tensor", lambda E, m=m, db=db: E.transpose(
                    pT[:, 128 * m:128 * (m + 1)], xnt[:, 128 * db:128 * (db + 1)],
                    id_sb[:]), inc=(m == 7))
            t_last = ("t", sch.n["t"])
            sch.wait("vector", t_last)
            sch.op("vector", lambda E, i=i, half=half: E.tensor_copy(
                xnf3[:, 8 * half:8 * half + 8, 128 * i:128 * (i + 1)], pT3[:, :, :]))
            pT_evac = ("v", sch.n["v"])
        xnt_free = t_last

    sch.wait("gpsimd", pT_evac)
    for db in range(16):
        sch.dma("gpsimd", "gd", lambda E, db=db: E.dma_start(
            out=xn_scr[128 * db:128 * (db + 1), :], in_=xnf[:, S * db:S * (db + 1)]))
    tok_xnscr = ("gd", sch.n["gd"])

    sflat = []
    ntile_s = (FLAT + 128 * SROWS - 1) // (128 * SROWS)
    for wi, wt in enumerate((w1, w3, w2)):
        fl = wt.rearrange("a b -> (a b)")
        for ti in range(ntile_s):
            lo = ti * 128 * SROWS
            n_el = min(128 * SROWS, FLAT - lo)
            ncols = n_el // 128
            ap = fl[lo:lo + 128 * ncols].rearrange("(p f) -> p f", p=128)
            sflat.append((ap, ncols))
    NS = len(sflat)
    ws_free = [None, None]
    wq32_free0 = sq_free
    s_dma_i = 0
    s_red_i = 0

    s_tok = [None] * NS

    def emit_s_dma():
        nonlocal s_dma_i
        if s_dma_i >= NS:
            return
        i = s_dma_i
        b = i % 2
        ap, ncols = sflat[i]
        sch.wait("gpsimd", ws_free[b])
        s_tok[i] = sch.dma("gpsimd", "g2" if b == 0 else "g3",
                           lambda E, ap=ap, b=b, ncols=ncols: E.dma_start(
            out=ws[b][:, 0:ncols], in_=ap))
        s_dma_i += 1

    def emit_s_reduce():
        nonlocal s_red_i
        if s_red_i >= NS:
            return
        emit_s_dma()
        i = s_red_i
        b = i % 2
        ncols = sflat[i][1]
        sch.wait("vector", s_tok[i])
        sch.op("vector", lambda E, b=b, ncols=ncols, i=i: E.tensor_reduce(
            sacw[:, i:i + 1], ws[b][:, 0:ncols], AX.X, ALU.add,
            apply_absolute_value=True))
        ws_free[b] = ("v", sch.n["v"])
        s_red_i += 1

    emit_s_dma()

    cts = []
    for h in range(NH):
        cts += [("K", 16 + h), ("V", 32 + h), ("Q", h)]
    wq32_free = [wq32_free0, None]
    wq16_free = [None, None]
    wqd_tok = [None, None]

    def emit_wq_dma(ci):
        if ci >= len(cts):
            return
        s = ci % 2
        gct = cts[ci][1]
        sch.wait("sync", wq32_free[s])
        w3v = wq32[s].ap().rearrange("p (db c) -> p db c", db=16)
        wqd_tok[s] = sch.dma("sync", "sa" if s == 0 else "sb",
                             lambda E, gct=gct, w3v=w3v: E.dma_start(
            out=w3v, in_=wqkv_r[:, :, 128 * gct:128 * (gct + 1)]))

    emit_wq_dma(0)
    psAB_free = [None, None]
    asm_free = [None, None]
    pE_free = None
    scr_tok = {}
    pFa = pF[:, 0:512]
    pFb = pF[:, 512:1024]
    abc = 0
    for ci, (kind, gct) in enumerate(cts):
        b2 = ci % 2
        emit_wq_dma(ci + 1)
        sch.wait("vector", wqd_tok[b2])
        sch.wait("vector", wq16_free[b2])
        sch.op("vector", lambda E, b2=b2: E.tensor_copy(wq16[b2][:], wq32[b2][:]))
        wq32_free[b2] = ("v", sch.n["v"])
        tok_cast = wq32_free[b2]
        if kind in ("K", "V"):
            h = gct - 16 if kind == "K" else gct - 32
            scr = k_scr if kind == "K" else v_scr
            for half in range(2):
                ab = abc % 2
                abc += 1
                sch.wait("tensor", tok_cast)
                sch.wait("tensor", pE_free)
                for db in range(16):
                    for tq in range(4):
                        tc = 4 * half + tq
                        sch.op("tensor", lambda E, b2=b2, db=db, tc=tc, tq=tq: E.matmul(
                            pE[:, 512 * tq:512 * (tq + 1)],
                            wq16[b2][:, 128 * db:128 * (db + 1)],
                            xnf[:, S * db + 512 * tc:S * db + 512 * (tc + 1)],
                            start=(db == 0), stop=(db == 15)),
                            inc=(db == 15 and tq == 3))
                t_mm = ("t", sch.n["t"])
                sch.wait("scalar", t_mm)
                sch.wait("scalar", asm_free[ab])
                sch.op("scalar", lambda E, ab=ab, gct=gct: E.activation(
                    asm[ab][:], pE[:, 0:2048], AF.Identity,
                    bias=bq_sb[:, gct:gct + 1]))
                pE_free = ("c", sch.n["c"])
                sch.wait("gpsimd", pE_free)
                sch.dma("gpsimd", "gd", lambda E, ab=ab, half=half, scr=scr, h=h: E.dma_start(
                    out=scr[128 * h:128 * (h + 1), 2048 * half:2048 * (half + 1)],
                    in_=asm[ab][:]))
                asm_free[ab] = ("gd", sch.n["gd"])
            scr_tok[(kind, h)] = ("gd", sch.n["gd"])
        else:
            h = gct
            ab = abc % 2
            abc += 1
            sch.wait("tensor", tok_cast)
            sch.wait("tensor", pE_free)
            for db in range(16):
                for qg in range(4):
                    sch.op("tensor", lambda E, b2=b2, db=db, qg=qg: E.matmul(
                        pE[:, 512 * qg:512 * (qg + 1)],
                        wq16[b2][:, 128 * db:128 * (db + 1)],
                        xnf[:, S * db + 512 * qg:S * db + 512 * (qg + 1)],
                        start=(db == 0), stop=(db == 15)),
                        inc=(db == 15 and qg == 3))
            t_mm = ("t", sch.n["t"])
            sch.wait("scalar", t_mm)
            sch.wait("scalar", asm_free[ab])
            sch.op("scalar", lambda E, ab=ab, h=h: E.activation(
                asm[ab][:], pE[:, 0:2048], AF.Identity,
                scale=1.0 / 128.0, bias=bq_q[:, h:h + 1]))
            pE_free = ("c", sch.n["c"])
            sch.wait("gpsimd", pE_free)
            sch.dma("gpsimd", "gd", lambda E, ab=ab, h=h: E.dma_start(
                out=q_scr[128 * h:128 * (h + 1), :], in_=asm[ab][:]))
            asm_free[ab] = ("gd", sch.n["gd"])
            scr_tok[("Q", h)] = ("gd", sch.n["gd"])
        wq16_free[b2] = t_mm
        for _ in range(11):
            emit_s_reduce()
    while s_red_i < NS:
        emit_s_reduce()

    nts = NS // 3
    for wi in range(3):
        sch.op("vector", lambda E, wi=wi, nts=nts: E.tensor_reduce(
            sacc[:, wi:wi + 1], sacw[:, nts * wi:nts * (wi + 1)], AX.X, ALU.add),
            inc=(wi == 2))

    tok_hia = ("v", sch.n["v"])
    sch.wait("tensor", pE_free)
    sch.wait("tensor", tok_hia)
    sch.op("tensor", lambda E: E.matmul(pE[0:1, 0:512], onescf_sb[:],
                                        sacc[:, 0:512], start=True, stop=True))
    tok_s1 = ("t", sch.n["t"])
    sch.wait("vector", tok_s1)
    sch.op("vector", lambda E: E.tensor_copy(ssml[:], pE[0:1, 0:512]), inc=False)
    sch.op("vector", lambda E: E.tensor_scalar_mul(ssml[0:1, 0:8], ssml[0:1, 0:8],
                                                   1.0 / FLAT))
    tok_s2 = ("v", sch.n["v"])
    sch.wait("tensor", tok_s2)
    sch.op("tensor", lambda E: E.matmul(pE[:, 512:520], ones1f_sb[:],
                                        ssml[0:1, 0:8], start=True, stop=True))
    tok_s3 = ("t", sch.n["t"])
    sch.wait("vector", tok_s3)
    sch.op("vector", lambda E: E.tensor_copy(sfin[:], pE[:, 512:1024]), inc=False)
    sch.op("vector", lambda E: E.tensor_copy(srep[:, 0:8], sfin[:, 0:8]), inc=False)
    sch.op("vector", lambda E: E.tensor_scalar(
        srepP[:, 0:8], sfin[:, 0:8], 0.5, None, ALU.mult), inc=False)
    sch.op("vector", lambda E: E.tensor_scalar(
        srepN[:, 0:8], sfin[:, 0:8], -0.5, None, ALU.mult), inc=False)
    sch.op("vector", lambda E: E.tensor_tensor(
        srepM[:, 0:1], sfin[:, 2:3], sfin[:, 1:2], ALU.mult))
    tok_sdone = ("v", sch.n["v"])
    sch.barrier()
    st01.close()

    st2 = ExitStack()
    e2 = st2.enter_context
    Ksb = [e2(nc.sbuf_tensor(f"Ksb{i}", [P, S], BF16)) for i in range(2)]
    Vsb = [e2(nc.sbuf_tensor(f"Vsb{i}", [P, S], BF16)) for i in range(2)]
    Qsb = [e2(nc.sbuf_tensor(f"Qsb{i}", [P, 2048], BF16)) for i in range(2)]
    Vtok = [e2(nc.sbuf_tensor(f"Vtok{i}", [P, S], BF16)) for i in range(2)]
    attn = [e2(nc.sbuf_tensor(f"attn{i}", [P, 2048], BF16)) for i in range(2)]
    ptb_ = [e2(nc.sbuf_tensor(f"pt{i}", [P, 512], BF16)) for i in range(2)]
    qbh = [e2(nc.sbuf_tensor(f"qbh{i}", [1, D], BF16)) for i in range(2)]
    rden = e2(nc.sbuf_tensor("rden", [1, 512], F32))
    rrep = e2(nc.sbuf_tensor("rrep", [P, 512], F32))

    KQ_free = [None, None]
    Vsb_free = [None, None]
    attn_free = [None, None]
    Vtok_free = [None, None]
    pt_free = [None, None]
    exp_read = [pE_free, pE_free]
    pT_ev = pT_evac
    av_free = tok_sdone
    den_free = tok_sdone
    repl_free = tok_sdone
    rden_free = None
    pC = pF[:, 1024:1536]
    ptc = 0

    hd_tok = [None] * NH

    def emit_head_dmas(h):
        hb = h % 2
        sch.wait("sync", scr_tok[("K", h)])
        sch.wait("sync", scr_tok[("V", h)])
        sch.wait("sync", scr_tok[("Q", h)])
        sch.wait("sync", KQ_free[hb])
        sch.wait("sync", Vsb_free[hb])
        sch.dma("sync", "sa" if hb == 0 else "sb", lambda E, hb=hb, h=h: E.dma_start(
            out=Ksb[hb][:], in_=k_scr[128 * h:128 * (h + 1), :]))
        sch.dma("sync", "sa" if hb == 0 else "sb", lambda E, hb=hb, h=h: E.dma_start(
            out=Vsb[hb][:], in_=v_scr[128 * h:128 * (h + 1), :]))
        sch.dma("sync", "sa" if hb == 0 else "sb", lambda E, hb=hb, h=h: E.dma_start(
            out=Qsb[hb][:], in_=q_scr[128 * h:128 * (h + 1), :]))
        hd_tok[h] = sch.dma("sync", "sa" if hb == 0 else "sb",
                            lambda E, hb=hb, h=h: E.dma_start(
            out=qbh[hb][:], in_=qbias[h:h + 1, :]))

    for _ in range(16):
        sch.op("scalar", lambda E: E.activation(jnk[:, 0:128], mtri_sb[:], AF.Exp),
               inc=False)
    emit_head_dmas(0)
    for h in range(NH):
        hb = h % 2
        if h + 1 < NH:
            emit_head_dmas(h + 1)
        t_vt = None
        sch.wait("tensor", hd_tok[h])
        for vb in range(4):
            sch.wait("tensor", pT_ev)
            for m in range(8):
                kt = 8 * vb + m
                sch.op("tensor", lambda E, hb=hb, m=m, kt=kt: E.transpose(
                    pT[:, 128 * m:128 * (m + 1)],
                    Vsb[hb][:, 128 * kt:128 * (kt + 1)], id_sb[:]), inc=(m == 7))
            t_vt = ("t", sch.n["t"])
            sch.wait("vector", t_vt)
            if vb == 0:
                sch.wait("vector", Vtok_free[hb])
            sch.op("vector", lambda E, hb=hb, vb=vb: E.tensor_copy(
                Vtok[hb][:, 1024 * vb:1024 * (vb + 1)], pT[:, :]))
            pT_ev = ("v", sch.n["v"])
        Vsb_free[hb] = t_vt
        t_avden = None
        v_attn = None
        sbp = 0

        def finish_gi(gi):
            nonlocal den_free, rden_free, repl_free, av_free, v_attn
            sch.wait("vector", t_avden)
            sch.wait("vector", rden_free)
            sch.op("vector", lambda E: E.reciprocal(rden[:], pE[0:1, 0:512]))
            v_rec = ("v", sch.n["v"])
            den_free = v_rec
            sch.wait("tensor", v_rec)
            sch.wait("tensor", repl_free)
            sch.op("tensor", lambda E: E.matmul(pE[:, 512:1024], ones1f_sb[:],
                                                rden[:], start=True, stop=True))
            t_repl = ("t", sch.n["t"])
            rden_free = t_repl
            sch.wait("vector", t_repl)
            sch.op("vector", lambda E: E.tensor_copy(rrep[:], pE[:, 512:1024]))
            repl_free = ("v", sch.n["v"])
            if gi == 0:
                sch.wait("vector", attn_free[hb])
            sch.op("vector", lambda E, hb=hb, gi=gi: E.tensor_tensor(
                attn[hb][:, 512 * gi:512 * (gi + 1)], pC[:, 0:512], rrep[:],
                ALU.mult))
            v_attn = ("v", sch.n["v"])
            av_free = v_attn

        def emit_scores(gi, kt):
            nonlocal sbp, ptc
            sub = kt if kt < 16 else kt - 16
            qa = max(4 * gi, sub)
            qoff = 128 * (qa - 4 * gi)
            base = pFa if sbp == 0 else pFb
            reg = base[:, qoff:512]
            my_sbp = sbp
            sch.wait("tensor", exp_read[sbp])
            sch.op("tensor", lambda E, hb=hb, gi=gi, qoff=qoff, reg=reg: E.matmul(
                reg, ones1_sb[:],
                qbh[hb][0:1, 512 * gi + qoff:512 * (gi + 1)],
                start=True, stop=False), inc=False)
            sch.op("tensor", lambda E, hb=hb, kt=kt, gi=gi, qoff=qoff, reg=reg: E.matmul(
                reg, Ksb[hb][:, 128 * kt:128 * (kt + 1)],
                Qsb[hb][:, 512 * gi + qoff:512 * (gi + 1)],
                start=False, stop=True))
            t_sc = ("t", sch.n["t"])
            masked = 4 * gi <= sub <= 4 * gi + 3
            v_mask = None
            if masked:
                msb = mtri_sb if kt < 16 else moth_sb
                off = 128 * (sub - 4 * gi)
                mreg = base[:, off:off + 128]
                sch.wait("vector", t_sc)
                sch.op("vector", lambda E, mreg=mreg, msb=msb: E.tensor_tensor(
                    mreg, mreg, msb[:], ALU.add))
                v_mask = ("v", sch.n["v"])
            pb = ptc % 2
            ptc += 1
            sch.wait("scalar", t_sc)
            if masked:
                sch.wait("scalar", v_mask)
            sch.wait("scalar", pt_free[pb])
            sch.op("scalar", lambda E, pb=pb, qoff=qoff, reg=reg, h=h, kt=kt: E.activation(
                ptb_[pb][:, qoff:512], reg, AF.Exp,
                bias=expb_sb[:, 32 * h + kt:32 * h + kt + 1]))
            c_exp = ("c", sch.n["c"])
            exp_read[my_sbp] = c_exp
            sbp ^= 1
            return (kt, pb, qoff, c_exp)

        def emit_avden(st, idx, nk):
            nonlocal t_avden
            kt, pb, qoff, c_exp = st
            sch.wait("tensor", c_exp)
            if idx == 0:
                sch.wait("tensor", av_free)
                sch.wait("tensor", den_free)
            sch.op("tensor", lambda E, hb=hb, kt=kt, pb=pb, qoff=qoff, idx=idx, nk=nk: E.matmul(
                pC[:, qoff:512], Vtok[hb][:, 128 * kt:128 * (kt + 1)],
                ptb_[pb][:, qoff:512], start=(idx == 0), stop=(idx == nk - 1)),
                inc=False)
            sch.op("tensor", lambda E, pb=pb, qoff=qoff, idx=idx, nk=nk: E.matmul(
                pE[0:1, qoff:512], onesc_sb[:], ptb_[pb][:, qoff:512],
                start=(idx == 0), stop=(idx == nk - 1)))
            t_avden = ("t", sch.n["t"])
            pt_free[pb] = t_avden

        all_kts = []
        for gi in range(4):
            kts = list(range(0, 4 * gi + 4)) + list(range(16, 16 + 4 * gi + 4))
            for idx, kt in enumerate(kts):
                all_kts.append((gi, kt, idx, len(kts)))
        staged = None
        for step, (gi, kt, idx, nk) in enumerate(all_kts):
            st_new = emit_scores(gi, kt)
            if staged is not None:
                pgi, pidx, pnk, pst = staged
                emit_avden(pst, pidx, pnk)
                if pidx == pnk - 1:
                    finish_gi(pgi)
            staged = (gi, idx, nk, st_new)
        pgi, pidx, pnk, pst = staged
        emit_avden(pst, pidx, pnk)
        finish_gi(pgi)
        KQ_free[hb] = t_avden
        Vtok_free[hb] = t_avden
        sch.wait("gpsimd", v_attn)
        sch.dma("gpsimd", "gd", lambda E, hb=hb, h=h: E.dma_start(
            out=att_scr[128 * h:128 * (h + 1), :], in_=attn[hb][:, 0:2048]))
        attn_free[hb] = ("gd", sch.n["gd"])
    att_all = ("gd", sch.n["gd"])
    sch.barrier()
    st2.close()

    st3a = ExitStack()
    e3 = st3a.enter_context
    w32 = [e3(nc.sbuf_tensor(f"w32_{i}", [P, D], F32)) for i in range(3)]
    uq = e3(nc.sbuf_tensor("uq", [P, D], BF16))
    tq = [e3(nc.sbuf_tensor(f"tq{i}", [P, D], BF16)) for i in range(2)]
    wstage = [e3(nc.sbuf_tensor(f"wstage{i}", [P, D], BF16)) for i in range(2)]
    w2_r = w2.rearrange("(db p) h -> p db h", p=128)

    jobs = [("w1", j) for j in range(NHT)] + [("w3", j) for j in range(NHT)] \
        + [("w2", j) for j in range(NHT)] + [("wp", j) for j in range(16)]
    scr_of = {"w1": w1t_scr, "w3": w3t_scr, "w2": w2t_scr}
    thr_of = {"w1": 0, "w3": 1, "w2": 2}
    w32_free = [None, None, None]
    tq_free = [None, None]
    wst_free = [None, None]
    pT_ev3 = pT_ev

    w_tok = [None] * len(jobs)

    def emit_w_dma(k):
        kind, j = jobs[k]
        bw = k % 3
        skey = ("sa", "sb", "sc")[bw]
        pad = kind in ("w1", "w3", "w2") and j == NHT - 1
        nrows = HID - 128 * j if pad else 128
        if pad:
            sch.wait("vector", w32_free[bw])
            sch.op("vector", lambda E, bw=bw: E.memset(w32[bw][:], 0.0))
            w32_free[bw] = ("v", sch.n["v"])
        sch.wait("sync", w32_free[bw])
        if kind == "w2":
            w32_3 = w32[bw].ap().rearrange("p (db c) -> p db c", db=16)
            w_tok[k] = sch.dma("sync", skey, lambda E, j=j, nrows=nrows, w32_3=w32_3: E.dma_start(
                out=w32_3[:, :, 0:nrows], in_=w2_r[:, :, 128 * j:128 * j + nrows]))
        elif kind == "wp":
            w_tok[k] = sch.dma("sync", skey, lambda E, bw=bw, j=j: E.dma_start(
                out=w32[bw][:], in_=wproj[128 * j:128 * (j + 1), :]))
        else:
            wt = w1 if kind == "w1" else w3
            w_tok[k] = sch.dma("sync", skey, lambda E, bw=bw, j=j, nrows=nrows, wt=wt: E.dma_start(
                out=w32[bw][0:nrows, :], in_=wt[128 * j:128 * j + nrows, :]))

    for _ in range(16):
        sch.op("scalar", lambda E: E.activation(jnk[:, 0:128], mtri_sb[:], AF.Silu),
               inc=False)
    emit_w_dma(0)
    emit_w_dma(1)
    wscr_done = {}
    for k, (kind, j) in enumerate(jobs):
        b = k % 2
        bw = k % 3
        if k + 2 < len(jobs):
            emit_w_dma(k + 2)
        if kind == "wp":
            sch.wait("vector", w_tok[k])
            sch.wait("vector", wst_free[b])
            sch.op("vector", lambda E, b=b, bw=bw: E.tensor_copy(wstage[b][:], w32[bw][:]))
            w32_free[bw] = ("v", sch.n["v"])
            sch.wait("gpsimd", w32_free[bw])
            sch.dma("gpsimd", "gd", lambda E, b=b, j=j: E.dma_start(
                out=wp_scr[128 * j:128 * (j + 1), :], in_=wstage[b][:]))
            wst_free[b] = ("gd", sch.n["gd"])
            continue
        tc_ = thr_of[kind]
        sch.wait("vector", w_tok[k])
        sch.wait("vector", tok_sdone)
        sch.op("vector", lambda E, bw=bw, tc_=tc_: E.tensor_single_scalar(
            uq[:], w32[bw][:], srepN[:, tc_:tc_ + 1], ALU.is_le), inc=False)
        sch.wait("vector", tq_free[b])
        sch.op("vector", lambda E, b=b, bw=bw, tc_=tc_: E.scalar_tensor_tensor(
            tq[b][:], w32[bw][:], srepP[:, tc_:tc_ + 1], uq[:],
            ALU.is_ge, ALU.subtract))
        w32_free[bw] = ("v", sch.n["v"])
        v_q = ("v", sch.n["v"])
        t_tr = None
        for half in range(2):
            sch.wait("tensor", v_q)
            sch.wait("tensor", pT_ev3)
            for m in range(8):
                db = 8 * half + m
                sch.op("tensor", lambda E, b=b, m=m, db=db: E.transpose(
                    pT[:, 128 * m:128 * (m + 1)],
                    tq[b][:, 128 * db:128 * (db + 1)], id_sb[:]), inc=(m == 7))
            t_tr = ("t", sch.n["t"])
            sch.wait("vector", t_tr)
            if half == 0:
                sch.wait("vector", wst_free[b])
            sch.op("vector", lambda E, b=b, half=half: E.tensor_copy(
                wstage[b][:, 1024 * half:1024 * (half + 1)], pT[:, :]))
            pT_ev3 = ("v", sch.n["v"])
        tq_free[b] = t_tr
        sch.wait("gpsimd", pT_ev3)
        sch.dma("gpsimd", "gd", lambda E, b=b, j=j, scr=scr_of[kind]: E.dma_start(
            out=scr[128 * j:128 * (j + 1), :], in_=wstage[b][:]))
        wst_free[b] = ("gd", sch.n["gd"])
        if j == NHT - 1:
            wscr_done[kind] = ("gd", sch.n["gd"])
    wscr_done["wp"] = ("gd", sch.n["gd"])
    sch.barrier()
    st3a.close()

    psF_ctx.__exit__(None, None, None)
    psT_ctx.__exit__(None, None, None)
    psG_ctx = nc.psum_tensor("psG", [P, 2048], F32)
    pG = psG_ctx.__enter__()

    st3b = ExitStack()
    e3b = st3b.enter_context
    g_sb = e3b(nc.sbuf_tensor("g_sb", [P, NHT * TCH], BF16))
    xnch = e3b(nc.sbuf_tensor("xnch", [P, 16 * TCH], BF16))
    attch = e3b(nc.sbuf_tensor("attch", [P, 16 * TCH], BF16))
    w1qT = [e3b(nc.sbuf_tensor(f"w1qT{i}", [P, D], BF16)) for i in range(2)]
    w3qT = [e3b(nc.sbuf_tensor(f"w3qT{i}", [P, D], BF16)) for i in range(2)]
    w2qT = [e3b(nc.sbuf_tensor(f"w2qT{i}", [P, D], BF16)) for i in range(2)]
    wpt = [e3b(nc.sbuf_tensor(f"wpt{i}", [P, D], BF16)) for i in range(2)]
    xtile = e3b(nc.sbuf_tensor("xtile", [P, D], F32))
    outst = e3b(nc.sbuf_tensor("outst", [P, D], F32))
    silu_t = e3b(nc.sbuf_tensor("silu_t", [P, TCH], F32))

    w1qT_free = [None, None]
    w3qT_free = [None, None]
    w2qT_free = [None, None]
    wpt_free = [None, None]
    xtile_free = None
    outst_free = None
    xnch_free = None
    attch_free = None
    g_free = None
    pG_free = [None, None]
    silu_free = None
    pE_out_free = repl_free
    pG_out_free = None

    f1_tok = [None] * NHT
    ch_tok = [None] * NCH

    def emit_f1_dmas(j):
        b = j % 2
        key = "sa" if b == 0 else "sb"
        sch.wait("sync", w1qT_free[b])
        sch.dma("sync", key, lambda E, b=b, j=j: E.dma_start(
            out=w1qT[b][:], in_=w1t_scr[128 * j:128 * (j + 1), :]))
        sch.wait("sync", w3qT_free[b])
        f1_tok[j] = sch.dma("sync", key, lambda E, b=b, j=j: E.dma_start(
            out=w3qT[b][:], in_=w3t_scr[128 * j:128 * (j + 1), :]))

    def emit_chunk_dmas(c):
        sch.wait("sync", xnch_free)
        for db in range(16):
            sch.dma("sync", "sd", lambda E, db=db, c=c: E.dma_start(
                out=xnch[:, TCH * db:TCH * (db + 1)],
                in_=xn_scr[128 * db:128 * (db + 1), TCH * c:TCH * (c + 1)]))
        sch.wait("sync", attch_free)
        for hc in range(16):
            sch.dma("sync", "sd", lambda E, hc=hc, c=c: E.dma_start(
                out=attch[:, TCH * hc:TCH * (hc + 1)],
                in_=att_scr[128 * hc:128 * (hc + 1), TCH * c:TCH * (c + 1)]))
        ch_tok[c] = ("sd", sch.n["sd"])

    emit_chunk_dmas(0)
    emit_f1_dmas(0)
    for c in range(NCH):
        for j in range(NHT):
            b = j % 2
            base = 1024 * (j % 2)
            if j + 1 < NHT:
                emit_f1_dmas(j + 1)
            sch.wait("tensor", f1_tok[j])
            sch.wait("tensor", ch_tok[c])
            if j == 0:
                sch.wait("tensor", g_free)
                sch.wait("tensor", pG_out_free)
            sch.wait("tensor", pG_free[j % 2])
            for db in range(16):
                sch.op("tensor", lambda E, b=b, db=db, base=base: E.matmul(
                    pG[:, base:base + 512], w1qT[b][:, 128 * db:128 * (db + 1)],
                    xnch[:, TCH * db:TCH * (db + 1)],
                    start=(db == 0), stop=(db == 15)), inc=False)
                sch.op("tensor", lambda E, b=b, db=db, base=base: E.matmul(
                    pG[:, base + 512:base + 1024], w3qT[b][:, 128 * db:128 * (db + 1)],
                    xnch[:, TCH * db:TCH * (db + 1)],
                    start=(db == 0), stop=(db == 15)), inc=(db == 15))
            t_h3 = ("t", sch.n["t"])
            t_h1 = t_h3
            w1qT_free[b] = t_h3
            w3qT_free[b] = t_h3
            sch.wait("scalar", t_h1)
            sch.wait("scalar", silu_free)
            sch.op("scalar", lambda E, base=base: E.activation(
                silu_t[:], pG[:, base:base + 512], AF.Silu, scale=srep[:, 0:1]))
            c_silu = ("c", sch.n["c"])
            sch.wait("vector", c_silu)
            if j == 0:
                sch.wait("vector", g_free)
            sch.op("vector", lambda E, j=j, base=base: E.scalar_tensor_tensor(
                g_sb[:, TCH * j:TCH * (j + 1)], silu_t[:], srepM[:, 0:1],
                pG[:, base + 512:base + 1024], ALU.mult, ALU.mult))
            pG_free[j % 2] = ("v", sch.n["v"])
            silu_free = pG_free[j % 2]
        v_g_all = ("v", sch.n["v"])
        xnch_free = ("t", sch.n["t"])
        for pair in range(2):
            tt = (2 * pair, 2 * pair + 1)
            ps_of = {tt[0]: pE, tt[1]: pG}
            for t in tt:
                ps = ps_of[t]
                if ps is pE:
                    sch.wait("tensor", pE_out_free)
                else:
                    sch.wait("tensor", v_g_all)
                    sch.wait("tensor", pG_free[0])
                    sch.wait("tensor", pG_free[1])
                    sch.wait("tensor", pG_out_free)
                for d in range(4):
                    sch.op("tensor", lambda E, ps=ps, d=d: E.matmul(
                        ps[:, 512 * d:512 * (d + 1)], ones1f_sb[:],
                        bproj_sb[0:1, 512 * d:512 * (d + 1)],
                        start=True, stop=False), inc=False)
            sch.wait("sync", wpt_free[0])
            if c == 0 and pair == 0:
                sch.wait("sync", wscr_done["wp"])
            wp_tok = [None] * 16
            wp_tok[0] = sch.dma("sync", "sa", lambda E: E.dma_start(
                out=wpt[0][:], in_=wp_scr[0:128, :]))
            for hc in range(16):
                bw = hc % 2
                if hc + 1 < 16:
                    bw2 = (hc + 1) % 2
                    sch.wait("sync", wpt_free[bw2])
                    wp_tok[hc + 1] = sch.dma("sync", "sa" if bw2 == 0 else "sb",
                                             lambda E, bw2=bw2, hc=hc: E.dma_start(
                        out=wpt[bw2][:], in_=wp_scr[128 * (hc + 1):128 * (hc + 2), :]))
                sch.wait("tensor", wp_tok[hc])
                if hc == 0:
                    sch.wait("tensor", ch_tok[c])
                for t in tt:
                    ps = ps_of[t]
                    for d in range(4):
                        sch.op("tensor", lambda E, ps=ps, hc=hc, d=d, t=t, bw=bw: E.matmul(
                            ps[:, 512 * d:512 * (d + 1)],
                            attch[:, TCH * hc + 128 * t:TCH * hc + 128 * (t + 1)],
                            wpt[bw][:, 512 * d:512 * (d + 1)],
                            start=False, stop=False), inc=(t == tt[1] and d == 3))
                wpt_free[bw] = ("t", sch.n["t"])
            sch.wait("sync", w2qT_free[0])
            if c == 0 and pair == 0:
                sch.wait("sync", wscr_done["w2"])
            w2_tok = [None] * NHT
            w2_tok[0] = sch.dma("sync", "sa", lambda E: E.dma_start(
                out=w2qT[0][:], in_=w2t_scr[0:128, :]))
            for j in range(NHT):
                b = j % 2
                if j + 1 < NHT:
                    b2_ = (j + 1) % 2
                    sch.wait("sync", w2qT_free[b2_])
                    w2_tok[j + 1] = sch.dma("sync", "sa" if b2_ == 0 else "sb",
                                            lambda E, b2_=b2_, j=j: E.dma_start(
                        out=w2qT[b2_][:], in_=w2t_scr[128 * (j + 1):128 * (j + 2), :]))
                sch.wait("tensor", w2_tok[j])
                if j == 0:
                    sch.wait("tensor", v_g_all)
                for t in tt:
                    ps = ps_of[t]
                    for d in range(4):
                        sch.op("tensor", lambda E, ps=ps, j=j, d=d, t=t, b=b: E.matmul(
                            ps[:, 512 * d:512 * (d + 1)],
                            g_sb[:, TCH * j + 128 * t:TCH * j + 128 * (t + 1)],
                            w2qT[b][:, 512 * d:512 * (d + 1)],
                            start=False, stop=(j == NHT - 1)),
                            inc=(t == tt[1] and d == 3))
                w2qT_free[b] = ("t", sch.n["t"])
            t_pair = ("t", sch.n["t"])
            g_free = t_pair
            for t in tt:
                ps = ps_of[t]
                sch.wait("sync", xtile_free)
                xt_tok2 = sch.dma("sync", "sd", lambda E, c=c, t=t: E.dma_start(
                    out=xtile[:], in_=x_ap[TCH * c + 128 * t:TCH * c + 128 * (t + 1), :]))
                sch.wait("vector", t_pair)
                sch.wait("vector", xt_tok2)
                sch.wait("vector", outst_free)
                sch.op("vector", lambda E, ps=ps: E.tensor_tensor(
                    outst[:], ps[:, 0:2048], xtile[:], ALU.add))
                v_ev = ("v", sch.n["v"])
                xtile_free = v_ev
                if ps is pE:
                    pE_out_free = v_ev
                else:
                    pG_out_free = v_ev
                sch.wait("gpsimd", v_ev)
                sch.dma("gpsimd", "gd", lambda E, c=c, t=t: E.dma_start(
                    out=out_ap[TCH * c + 128 * t:TCH * c + 128 * (t + 1), :],
                    in_=outst[:]))
                outst_free = ("gd", sch.n["gd"])
        attch_free = ("t", sch.n["t"])
        if c + 1 < NCH:
            emit_chunk_dmas(c + 1)
            emit_f1_dmas(0)
    final_gd = sch.n["gd"]
    st3b.close()
    psG_ctx.__exit__(None, None, None)
    psE_ctx.__exit__(None, None, None)

    sch.q["gpsimd"].append(lambda E, v=final_gd * 16: E.wait_ge(sch.sems["gd"], v))

    with nc.Block() as blk:
        blk.sync(lambda E: [f(E) for f in sch.q["sync"]])
        blk.tensor(lambda E: [f(E) for f in sch.q["tensor"]])
        blk.vector(lambda E: [f(E) for f in sch.q["vector"]])
        blk.scalar(lambda E: [f(E) for f in sch.q["scalar"]])
        blk.gpsimd(lambda E: [f(E) for f in sch.q["gpsimd"]])

    big.close()
    nc._sch_counts = dict(sch.n)
    return nc


_CACHE = {}


def _alibi_slopes(n):
    def pow2_slopes(n):
        start = 2.0 ** (-2.0 ** (-(math.log2(n) - 3)))
        return [start * start ** i for i in range(n)]
    if math.log2(n).is_integer():
        return pow2_slopes(n)
    cp2 = 2 ** math.floor(math.log2(n))
    return pow2_slopes(cp2) + _alibi_slopes(2 * cp2)[0::2][:n - cp2]


def _consts(p):
    slopes = np.asarray(_alibi_slopes(NH), np.float64)
    gk = np.zeros(S, np.int64)
    for j in range(32):
        gt = 2 * j + p if j < 16 else 2 * (j - 16) + (1 - p)
        gk[128 * j:128 * (j + 1)] = np.arange(128 * gt, 128 * gt + 128)
    expb = np.zeros((P, 512), np.float32)
    for h in range(NH):
        for kt in range(32):
            expb[:, 32 * h + kt] = slopes[h] * gk[128 * kt:128 * (kt + 1)]
    qb = np.zeros((NH, D), np.float64)
    for h in range(NH):
        qb[h, :] = -slopes[h] * gk[0:2048]
    qb = qb.astype(ml_dtypes.bfloat16)
    r = np.arange(P)
    mtri = np.where(r[:, None] > r[None, :], MASKNEG, 0.0).astype(np.float32)
    moth = (np.full((P, P), MASKNEG, np.float32) if p == 0
            else np.zeros((P, P), np.float32))
    return expb, qb, mtri, moth


def kernel(L0_attn_hidden_states, L0_attn_w_qkv, L0_attn_b_qkv, L0_attn_w_proj,
           L0_attn_b_proj, L0_ffn_w1, L0_ffn_w3, L0_ffn_w2):
    x = np.ascontiguousarray(np.asarray(L0_attn_hidden_states, np.float32))
    wqkv = np.ascontiguousarray(np.asarray(L0_attn_w_qkv, np.float32))
    bqkv = np.asarray(L0_attn_b_qkv, np.float32)
    wproj = np.ascontiguousarray(np.asarray(L0_attn_w_proj, np.float32))
    bpro = np.asarray(L0_attn_b_proj, np.float32)
    w1 = np.ascontiguousarray(np.asarray(L0_ffn_w1, np.float32))
    w3 = np.ascontiguousarray(np.asarray(L0_ffn_w3, np.float32))
    w2 = np.ascontiguousarray(np.asarray(L0_ffn_w2, np.float32))

    if "nc" not in _CACHE:
        _CACHE["nc"] = build_nc()
    nc = _CACHE["nc"]

    bqkvt = np.ascontiguousarray(bqkv.reshape(48, 128).T)
    bproj2 = np.ascontiguousarray(bpro.reshape(1, D))
    ident = np.eye(P).astype(ml_dtypes.bfloat16)
    ones1 = np.ones((1, P), ml_dtypes.bfloat16)
    ones1f = np.ones((1, P), np.float32)
    onesc = np.ones((P, 1), ml_dtypes.bfloat16)
    onescf = np.ones((P, 1), np.float32)

    perms = {}
    in_maps = []
    for core in range(8):
        b, p = core // 2, core % 2
        if p not in perms:
            own = np.concatenate(
                [np.arange(128 * (2 * i + p), 128 * (2 * i + p) + 128)
                 for i in range(16)])
            oth = np.concatenate(
                [np.arange(128 * (2 * i + 1 - p), 128 * (2 * i + 1 - p) + 128)
                 for i in range(16)])
            expb, qb, mtri, moth = _consts(p)
            perms[p] = (np.concatenate([own, oth]), own, expb, qb, mtri, moth)
        perm, own, expb, qb, mtri, moth = perms[p]
        xp = np.ascontiguousarray(x[b][perm])
        in_maps.append({
            "x": xp, "wqkv": wqkv, "bqkvt": bqkvt, "wproj": wproj,
            "bproj": bproj2, "w1": w1, "w3": w3, "w2": w2,
            "expb": expb, "qbias": qb, "mtri": mtri, "moth": moth,
            "ident": ident, "ones1": ones1, "ones1f": ones1f,
            "onesc": onesc, "onescf": onescf,
        })

    res = run_bass_kernel_spmd(nc, in_maps, core_ids=list(range(8)))

    out = np.empty((4, S, D), np.float32)
    for core in range(8):
        b, p = core // 2, core % 2
        out[b][perms[p][1]] = res.results[core]["out"]
    return out


# version 12 (final)
# speedup vs baseline: 1.0392x; 1.0392x over previous
import math
from contextlib import ExitStack

import numpy as np
import ml_dtypes

import concourse.bass as bass
import concourse.mybir as mybir
from concourse.bass_utils import run_bass_kernel_spmd

F32 = mybir.dt.float32
BF16 = mybir.dt.bfloat16
AF = mybir.ActivationFunctionType
ALU = mybir.AluOpType
AX = mybir.AxisListType

P = 128
D = 2048
S = 4096
NH = 16
NT = 32
HID = 5461
HPAD = 5504
NHT = 43
TCH = 512
NCH = 4
EPS = 1e-5
FLAT = HID * D
MASKNEG = -60000.0
SROWS = 512

ENGS = ("sync", "tensor", "vector", "scalar", "gpsimd")


class Sched:
    def __init__(self):
        self.q = {e: [] for e in ENGS}
        self.n = {"sd": 0, "gd": 0, "g2": 0, "g3": 0, "sa": 0, "sb": 0,
                  "sc": 0, "t": 0, "v": 0, "c": 0}
        self.sems = None
        self.waited = {e: {k: 0 for k in self.n} for e in ENGS}

    def dma(self, eng, key, fn):
        self.n[key] += 1

        def run(E, fn=fn, key=key):
            fn(E).then_inc(self.sems[key], 16)

        self.q[eng].append(run)
        return (key, self.n[key])

    def op(self, eng, fn, inc=True):
        key = {"tensor": "t", "vector": "v", "scalar": "c"}[eng]
        if inc:
            self.n[key] += 1

        def run(E, fn=fn, inc=inc, key=key):
            ins = fn(E)
            if inc:
                ins.then_inc(self.sems[key], 1)

        self.q[eng].append(run)
        return (key, self.n[key])

    def barrier(self):
        snap = [(k, v) for k, v in self.n.items()]
        for eng in ENGS:
            for k, v in snap:
                self.wait(eng, (k, v))

    def waitq(self, eng, key):
        self.wait(eng, (key, self.n[key]))

    def wait(self, eng, tok):
        if tok is None:
            return
        key, cnt = tok
        if cnt <= 0 or self.waited[eng][key] >= cnt:
            return
        self.waited[eng][key] = cnt
        val = cnt * 16 if key not in ("t", "v", "c") else cnt

        def run(E, key=key, val=val):
            E.wait_ge(self.sems[key], val)

        self.q[eng].append(run)


def build_nc():
    nc = bass.Bass("TRN2", target_bir_lowering=False)

    x_ap = nc.dram_tensor("x", [S, D], F32, kind="ExternalInput").ap()
    wqkv = nc.dram_tensor("wqkv", [D, 3 * D], F32, kind="ExternalInput").ap()
    bqkvt = nc.dram_tensor("bqkvt", [P, 48], F32, kind="ExternalInput").ap()
    wproj = nc.dram_tensor("wproj", [D, D], F32, kind="ExternalInput").ap()
    bproj = nc.dram_tensor("bproj", [1, D], F32, kind="ExternalInput").ap()
    w1 = nc.dram_tensor("w1", [HID, D], F32, kind="ExternalInput").ap()
    w3 = nc.dram_tensor("w3", [HID, D], F32, kind="ExternalInput").ap()
    w2 = nc.dram_tensor("w2", [D, HID], F32, kind="ExternalInput").ap()
    expb = nc.dram_tensor("expb", [P, 512], F32, kind="ExternalInput").ap()
    qbias = nc.dram_tensor("qbias", [NH, D], BF16, kind="ExternalInput").ap()
    mtri = nc.dram_tensor("mtri", [P, P], F32, kind="ExternalInput").ap()
    moth = nc.dram_tensor("moth", [P, P], F32, kind="ExternalInput").ap()
    ident = nc.dram_tensor("ident", [P, P], BF16, kind="ExternalInput").ap()
    ones1 = nc.dram_tensor("ones1", [1, P], BF16, kind="ExternalInput").ap()
    ones1f = nc.dram_tensor("ones1f", [1, P], F32, kind="ExternalInput").ap()
    onesc = nc.dram_tensor("onesc", [P, 1], BF16, kind="ExternalInput").ap()
    onescf = nc.dram_tensor("onescf", [P, 1], F32, kind="ExternalInput").ap()

    out_ap = nc.dram_tensor("out", [2048, D], F32, kind="ExternalOutput").ap()

    xn_scr = nc.dram_tensor("xn_scr", [D, S], BF16, kind="Internal").ap()
    k_scr = nc.dram_tensor("k_scr", [D, S], BF16, kind="Internal").ap()
    v_scr = nc.dram_tensor("v_scr", [D, S], BF16, kind="Internal").ap()
    q_scr = nc.dram_tensor("q_scr", [D, 2048], BF16, kind="Internal").ap()
    att_scr = nc.dram_tensor("att_scr", [D, 2048], BF16, kind="Internal").ap()
    w1t_scr = nc.dram_tensor("w1t_scr", [HPAD, D], BF16, kind="Internal").ap()
    w3t_scr = nc.dram_tensor("w3t_scr", [HPAD, D], BF16, kind="Internal").ap()
    w2t_scr = nc.dram_tensor("w2t_scr", [HPAD, D], BF16, kind="Internal").ap()
    wp_scr = nc.dram_tensor("wp_scr", [D, D], BF16, kind="Internal").ap()

    sch = Sched()
    big = ExitStack()
    en = big.enter_context
    sch.sems = {k: en(nc.semaphore("sem_" + k))
                for k in ("sd", "gd", "g2", "g3", "sa", "sb", "sc",
                          "t", "v", "c")}

    bq_sb = en(nc.sbuf_tensor("bq_sb", [P, 48], F32))
    bq_q = en(nc.sbuf_tensor("bq_q", [P, 16], F32))
    bproj_sb = en(nc.sbuf_tensor("bproj_sb", [1, D], F32))
    expb_sb = en(nc.sbuf_tensor("expb_sb", [P, 512], F32))
    mtri_sb = en(nc.sbuf_tensor("mtri_sb", [P, P], F32))
    moth_sb = en(nc.sbuf_tensor("moth_sb", [P, P], F32))
    id_sb = en(nc.sbuf_tensor("id_sb", [P, P], BF16))
    ones1_sb = en(nc.sbuf_tensor("ones1_sb", [1, P], BF16))
    ones1f_sb = en(nc.sbuf_tensor("ones1f_sb", [1, P], F32))
    onesc_sb = en(nc.sbuf_tensor("onesc_sb", [P, 1], BF16))
    onescf_sb = en(nc.sbuf_tensor("onescf_sb", [P, 1], F32))
    sacc = en(nc.sbuf_tensor("sacc", [P, 512], F32))

    sacw = en(nc.sbuf_tensor("sacw", [P, 520], F32))
    srep = en(nc.sbuf_tensor("srep", [P, 8], F32))
    srepP = en(nc.sbuf_tensor("srepP", [P, 8], F32))
    srepN = en(nc.sbuf_tensor("srepN", [P, 8], F32))
    srepM = en(nc.sbuf_tensor("srepM", [P, 2], F32))
    ssml = en(nc.sbuf_tensor("ssml", [1, 512], F32))
    sfin = en(nc.sbuf_tensor("sfin", [P, 512], F32))
    jnk = en(nc.sbuf_tensor("jnk", [P, 512], F32))


    psE_ctx = nc.psum_tensor("psE", [P, 2048], F32)
    pE = psE_ctx.__enter__()
    psT_ctx = nc.psum_tensor("psT", [P, 1024], BF16)
    pT = psT_ctx.__enter__()
    psF_ctx = nc.psum_tensor("psF", [P, 1536], F32)
    pF = psF_ctx.__enter__()
    pT3 = pT.ap().rearrange("p (m t) -> p m t", m=8)

    for dst, src in [
        (bq_sb[:], bqkvt), (bproj_sb[:], bproj), (expb_sb[:], expb),
        (mtri_sb[:], mtri), (moth_sb[:], moth),
        (id_sb[:], ident), (ones1_sb[:], ones1), (ones1f_sb[:], ones1f),
        (onesc_sb[:], onesc), (onescf_sb[:], onescf),
    ]:
        sch.dma("sync", "sd", lambda E, d=dst, s=src: E.dma_start(out=d, in_=s))
    sch.waitq("vector", "sd")
    sch.op("vector", lambda E: E.tensor_scalar_mul(bq_q[:, 0:16], bq_sb[:, 0:16],
                                                   1.0 / 128.0), inc=False)
    sch.op("vector", lambda E: E.memset(sacc[:], 0.0), inc=False)


    st01 = ExitStack()
    e01 = st01.enter_context
    xt = [e01(nc.sbuf_tensor(f"xt{i}", [P, D], F32)) for i in range(2)]
    st8 = e01(nc.sbuf_tensor("st8", [P, 16], F32))
    stA = e01(nc.sbuf_tensor("stA", [P, 2], F32))
    stB = e01(nc.sbuf_tensor("stB", [P, 2], F32))
    stM = e01(nc.sbuf_tensor("stM", [P, 2], F32))
    stE = e01(nc.sbuf_tensor("stE", [P, 2], F32))
    stM2 = e01(nc.sbuf_tensor("stM2", [P, 2], F32))
    stV = e01(nc.sbuf_tensor("stV", [P, 2], F32))
    stSq = e01(nc.sbuf_tensor("stSq", [P, 2], F32))
    stR = e01(nc.sbuf_tensor("stR", [P, 2], F32))
    stNB = e01(nc.sbuf_tensor("stNB", [P, 2], F32))
    stRn = e01(nc.sbuf_tensor("stRn", [P, 2], F32))
    epsc = e01(nc.sbuf_tensor("epsc", [P, 1], F32))
    xnt = e01(nc.sbuf_tensor("xnt", [P, D], BF16))
    xnf = e01(nc.sbuf_tensor("xnf", [P, 16 * S], BF16))
    wq32 = [e01(nc.sbuf_tensor(f"wq32_{i}", [P, D], F32)) for i in range(2)]
    sq32 = wq32[0]
    wq16 = [e01(nc.sbuf_tensor(f"wq16_{i}", [P, D], BF16)) for i in range(2)]
    asm = [e01(nc.sbuf_tensor(f"asm{i}", [P, D], BF16)) for i in range(2)]
    ws = [e01(nc.sbuf_tensor(f"ws{i}", [P, SROWS], F32)) for i in range(2)]
    xnf3 = xnf.ap().rearrange("p (db t) -> p db t", db=16)
    wqkv_r = wqkv.rearrange("(db p) c -> p db c", p=128)

    sch.waitq("tensor", "sd")
    sch.waitq("gpsimd", "sd")
    sch.waitq("scalar", "sd")
    for _ in range(16):
        sch.op("scalar", lambda E: E.activation(jnk[:], expb_sb[:], AF.Sqrt),
               inc=False)
    xt_free = [None, None]
    xnt_free = None
    pT_evac = None
    sq_free = None

    xt_tok = [None, None]

    def emit_x_dma(i):
        b = i % 2
        sch.wait("sync", xt_free[b])
        xt_tok[b] = sch.dma("sync", "sa" if b == 0 else "sb",
                            lambda E, i=i, b=b: E.dma_start(
            out=xt[b][:], in_=x_ap[128 * i:128 * (i + 1), :]))

    sch.op("vector", lambda E: E.memset(epsc[:], EPS))
    emit_x_dma(0)
    emit_x_dma(1)
    for i in range(NT):
        b = i % 2
        sch.wait("scalar", xt_tok[b])
        sch.wait("scalar", sq_free)
        sch.op("scalar", lambda E, b=b: E.activation(sq32[:], xt[b][:], AF.Square))
        tok_sq = ("c", sch.n["c"])
        sch.wait("vector", xt_tok[b])
        sch.op("vector", lambda E, b=b: E.tensor_reduce(
            stA[:, b:b + 1], xt[b][:], AX.X, ALU.add), inc=False)
        sch.wait("vector", tok_sq)
        sch.op("vector", lambda E, b=b: E.tensor_reduce(
            stB[:, b:b + 1], sq32[:], AX.X, ALU.add))
        sq_free = ("v", sch.n["v"])
        tok_red = ("v", sch.n["v"])
        sch.wait("scalar", tok_red)
        sch.op("scalar", lambda E, b=b: E.activation(
            stM[:, b:b + 1], stA[:, b:b + 1], AF.Identity,
            scale=-1.0 / D), inc=False)
        sch.op("scalar", lambda E, b=b: E.activation(
            stE[:, b:b + 1], stB[:, b:b + 1], AF.Identity,
            scale=1.0 / D, bias=epsc[:, 0:1]), inc=False)
        sch.op("scalar", lambda E, b=b: E.activation(
            stM2[:, b:b + 1], stM[:, b:b + 1], AF.Square), inc=False)
        sch.op("scalar", lambda E, b=b: E.activation(
            stV[:, b:b + 1], stM2[:, b:b + 1], AF.Identity,
            scale=-1.0, bias=stE[:, b:b + 1]), inc=False)
        sch.op("scalar", lambda E, b=b: E.activation(
            stSq[:, b:b + 1], stV[:, b:b + 1], AF.Sqrt))
        tok_sqrt = ("c", sch.n["c"])
        sch.wait("vector", tok_sqrt)
        sch.op("vector", lambda E, b=b: E.reciprocal(
            stR[:, b:b + 1], stSq[:, b:b + 1]))
        tok_r = ("v", sch.n["v"])
        sch.wait("scalar", tok_r)
        sch.op("scalar", lambda E, b=b: E.activation(
            stNB[:, b:b + 1], stM[:, b:b + 1], AF.Identity,
            scale=stR[:, b:b + 1]))
        tok_ln = ("c", sch.n["c"])
        sch.wait("scalar", tok_ln)
        sch.wait("scalar", xnt_free)
        sch.op("scalar", lambda E, b=b: E.activation(
            xnt[:], xt[b][:], AF.Identity,
            bias=stNB[:, b:b + 1], scale=stR[:, b:b + 1]))
        tok_app = ("c", sch.n["c"])
        xt_free[b] = tok_app
        if i + 2 < NT:
            emit_x_dma(i + 2)
        t_last = None
        for half in range(2):
            sch.wait("tensor", tok_app)
            sch.wait("tensor", pT_evac)
            for m in range(8):
                db = 8 * half + m
                sch.op("tensor", lambda E, m=m, db=db: E.transpose(
                    pT[:, 128 * m:128 * (m + 1)], xnt[:, 128 * db:128 * (db + 1)],
                    id_sb[:]), inc=(m == 7))
            t_last = ("t", sch.n["t"])
            sch.wait("vector", t_last)
            sch.op("vector", lambda E, i=i, half=half: E.tensor_copy(
                xnf3[:, 8 * half:8 * half + 8, 128 * i:128 * (i + 1)], pT3[:, :, :]))
            pT_evac = ("v", sch.n["v"])
        xnt_free = t_last

    sch.wait("gpsimd", pT_evac)
    for db in range(16):
        sch.dma("gpsimd", "gd", lambda E, db=db: E.dma_start(
            out=xn_scr[128 * db:128 * (db + 1), :], in_=xnf[:, S * db:S * (db + 1)]))
    tok_xnscr = ("gd", sch.n["gd"])

    sflat = []
    ntile_s = (FLAT + 128 * SROWS - 1) // (128 * SROWS)
    for wi, wt in enumerate((w1, w3, w2)):
        fl = wt.rearrange("a b -> (a b)")
        for ti in range(ntile_s):
            lo = ti * 128 * SROWS
            n_el = min(128 * SROWS, FLAT - lo)
            ncols = n_el // 128
            ap = fl[lo:lo + 128 * ncols].rearrange("(p f) -> p f", p=128)
            sflat.append((ap, ncols))
    NS = len(sflat)
    ws_free = [None, None]
    wq32_free0 = sq_free
    s_dma_i = 0
    s_red_i = 0

    s_tok = [None] * NS

    def emit_s_dma():
        nonlocal s_dma_i
        if s_dma_i >= NS:
            return
        i = s_dma_i
        b = i % 2
        ap, ncols = sflat[i]
        sch.wait("gpsimd", ws_free[b])
        s_tok[i] = sch.dma("gpsimd", "g2" if b == 0 else "g3",
                           lambda E, ap=ap, b=b, ncols=ncols: E.dma_start(
            out=ws[b][:, 0:ncols], in_=ap))
        s_dma_i += 1

    def emit_s_reduce():
        nonlocal s_red_i
        if s_red_i >= NS:
            return
        emit_s_dma()
        i = s_red_i
        b = i % 2
        ncols = sflat[i][1]
        sch.wait("vector", s_tok[i])
        sch.op("vector", lambda E, b=b, ncols=ncols, i=i: E.tensor_reduce(
            sacw[:, i:i + 1], ws[b][:, 0:ncols], AX.X, ALU.add,
            apply_absolute_value=True))
        ws_free[b] = ("v", sch.n["v"])
        s_red_i += 1

    emit_s_dma()

    cts = []
    for h in range(NH):
        cts += [("K", 16 + h), ("V", 32 + h), ("Q", h)]
    wq32_free = [wq32_free0, None]
    wq16_free = [None, None]
    wqd_tok = [None, None]

    def emit_wq_dma(ci):
        if ci >= len(cts):
            return
        s = ci % 2
        gct = cts[ci][1]
        sch.wait("sync", wq32_free[s])
        w3v = wq32[s].ap().rearrange("p (db c) -> p db c", db=16)
        wqd_tok[s] = sch.dma("sync", "sa" if s == 0 else "sb",
                             lambda E, gct=gct, w3v=w3v: E.dma_start(
            out=w3v, in_=wqkv_r[:, :, 128 * gct:128 * (gct + 1)]))

    emit_wq_dma(0)
    psAB_free = [None, None]
    asm_free = [None, None]
    pE_free = None
    scr_tok = {}
    pFa = pF[:, 0:512]
    pFb = pF[:, 512:1024]
    abc = 0
    for ci, (kind, gct) in enumerate(cts):
        b2 = ci % 2
        emit_wq_dma(ci + 1)
        sch.wait("vector", wqd_tok[b2])
        sch.wait("vector", wq16_free[b2])
        sch.op("vector", lambda E, b2=b2: E.tensor_copy(wq16[b2][:], wq32[b2][:]))
        wq32_free[b2] = ("v", sch.n["v"])
        tok_cast = wq32_free[b2]
        if kind in ("K", "V"):
            h = gct - 16 if kind == "K" else gct - 32
            scr = k_scr if kind == "K" else v_scr
            for half in range(2):
                ab = abc % 2
                abc += 1
                sch.wait("tensor", tok_cast)
                sch.wait("tensor", pE_free)
                for db in range(16):
                    for tq in range(4):
                        tc = 4 * half + tq
                        sch.op("tensor", lambda E, b2=b2, db=db, tc=tc, tq=tq: E.matmul(
                            pE[:, 512 * tq:512 * (tq + 1)],
                            wq16[b2][:, 128 * db:128 * (db + 1)],
                            xnf[:, S * db + 512 * tc:S * db + 512 * (tc + 1)],
                            start=(db == 0), stop=(db == 15)),
                            inc=(db == 15 and tq == 3))
                t_mm = ("t", sch.n["t"])
                sch.wait("scalar", t_mm)
                sch.wait("scalar", asm_free[ab])
                sch.op("scalar", lambda E, ab=ab, gct=gct: E.activation(
                    asm[ab][:], pE[:, 0:2048], AF.Identity,
                    bias=bq_sb[:, gct:gct + 1]))
                pE_free = ("c", sch.n["c"])
                sch.wait("gpsimd", pE_free)
                sch.dma("gpsimd", "gd", lambda E, ab=ab, half=half, scr=scr, h=h: E.dma_start(
                    out=scr[128 * h:128 * (h + 1), 2048 * half:2048 * (half + 1)],
                    in_=asm[ab][:]))
                asm_free[ab] = ("gd", sch.n["gd"])
            scr_tok[(kind, h)] = ("gd", sch.n["gd"])
        else:
            h = gct
            ab = abc % 2
            abc += 1
            sch.wait("tensor", tok_cast)
            sch.wait("tensor", pE_free)
            for db in range(16):
                for qg in range(4):
                    sch.op("tensor", lambda E, b2=b2, db=db, qg=qg: E.matmul(
                        pE[:, 512 * qg:512 * (qg + 1)],
                        wq16[b2][:, 128 * db:128 * (db + 1)],
                        xnf[:, S * db + 512 * qg:S * db + 512 * (qg + 1)],
                        start=(db == 0), stop=(db == 15)),
                        inc=(db == 15 and qg == 3))
            t_mm = ("t", sch.n["t"])
            sch.wait("scalar", t_mm)
            sch.wait("scalar", asm_free[ab])
            sch.op("scalar", lambda E, ab=ab, h=h: E.activation(
                asm[ab][:], pE[:, 0:2048], AF.Identity,
                scale=1.0 / 128.0, bias=bq_q[:, h:h + 1]))
            pE_free = ("c", sch.n["c"])
            sch.wait("gpsimd", pE_free)
            sch.dma("gpsimd", "gd", lambda E, ab=ab, h=h: E.dma_start(
                out=q_scr[128 * h:128 * (h + 1), :], in_=asm[ab][:]))
            asm_free[ab] = ("gd", sch.n["gd"])
            scr_tok[("Q", h)] = ("gd", sch.n["gd"])
        wq16_free[b2] = t_mm
        for _ in range(11):
            emit_s_reduce()
    while s_red_i < NS:
        emit_s_reduce()

    nts = NS // 3
    for wi in range(3):
        sch.op("vector", lambda E, wi=wi, nts=nts: E.tensor_reduce(
            sacc[:, wi:wi + 1], sacw[:, nts * wi:nts * (wi + 1)], AX.X, ALU.add),
            inc=(wi == 2))

    tok_hia = ("v", sch.n["v"])
    sch.wait("tensor", pE_free)
    sch.wait("tensor", tok_hia)
    sch.op("tensor", lambda E: E.matmul(pE[0:1, 0:512], onescf_sb[:],
                                        sacc[:, 0:512], start=True, stop=True))
    tok_s1 = ("t", sch.n["t"])
    sch.wait("vector", tok_s1)
    sch.op("vector", lambda E: E.tensor_copy(ssml[:], pE[0:1, 0:512]), inc=False)
    sch.op("vector", lambda E: E.tensor_scalar_mul(ssml[0:1, 0:8], ssml[0:1, 0:8],
                                                   1.0 / FLAT))
    tok_s2 = ("v", sch.n["v"])
    sch.wait("tensor", tok_s2)
    sch.op("tensor", lambda E: E.matmul(pE[:, 512:520], ones1f_sb[:],
                                        ssml[0:1, 0:8], start=True, stop=True))
    tok_s3 = ("t", sch.n["t"])
    sch.wait("vector", tok_s3)
    sch.op("vector", lambda E: E.tensor_copy(sfin[:], pE[:, 512:1024]), inc=False)
    sch.op("vector", lambda E: E.tensor_copy(srep[:, 0:8], sfin[:, 0:8]), inc=False)
    sch.op("vector", lambda E: E.tensor_scalar(
        srepP[:, 0:8], sfin[:, 0:8], 0.5, None, ALU.mult), inc=False)
    sch.op("vector", lambda E: E.tensor_scalar(
        srepN[:, 0:8], sfin[:, 0:8], -0.5, None, ALU.mult), inc=False)
    sch.op("vector", lambda E: E.tensor_tensor(
        srepM[:, 0:1], sfin[:, 2:3], sfin[:, 1:2], ALU.mult))
    tok_sdone = ("v", sch.n["v"])
    sch.barrier()
    st01.close()

    st2 = ExitStack()
    e2 = st2.enter_context
    Ksb = [e2(nc.sbuf_tensor(f"Ksb{i}", [P, S], BF16)) for i in range(2)]
    Vsb = [e2(nc.sbuf_tensor(f"Vsb{i}", [P, S], BF16)) for i in range(2)]
    Qsb = [e2(nc.sbuf_tensor(f"Qsb{i}", [P, 2048], BF16)) for i in range(2)]
    Vtok = [e2(nc.sbuf_tensor(f"Vtok{i}", [P, S], BF16)) for i in range(2)]
    attn = [e2(nc.sbuf_tensor(f"attn{i}", [P, 2048], BF16)) for i in range(2)]
    ptb_ = [e2(nc.sbuf_tensor(f"pt{i}", [P, 512], BF16)) for i in range(2)]
    qbh = [e2(nc.sbuf_tensor(f"qbh{i}", [1, D], BF16)) for i in range(2)]
    rden = e2(nc.sbuf_tensor("rden", [1, 512], F32))
    rrep = e2(nc.sbuf_tensor("rrep", [P, 512], F32))

    KQ_free = [None, None]
    Vsb_free = [None, None]
    attn_free = [None, None]
    Vtok_free = [None, None]
    pt_free = [None, None]
    exp_read = [pE_free, pE_free]
    pT_ev = pT_evac
    av_free = tok_sdone
    den_free = tok_sdone
    repl_free = tok_sdone
    rden_free = None
    pC = pF[:, 1024:1536]
    ptc = 0

    hd_tok = [None] * NH

    def emit_head_dmas(h):
        hb = h % 2
        sch.wait("sync", scr_tok[("K", h)])
        sch.wait("sync", scr_tok[("V", h)])
        sch.wait("sync", scr_tok[("Q", h)])
        sch.wait("sync", KQ_free[hb])
        sch.wait("sync", Vsb_free[hb])
        sch.dma("sync", "sa" if hb == 0 else "sb", lambda E, hb=hb, h=h: E.dma_start(
            out=Ksb[hb][:], in_=k_scr[128 * h:128 * (h + 1), :]))
        sch.dma("sync", "sa" if hb == 0 else "sb", lambda E, hb=hb, h=h: E.dma_start(
            out=Vsb[hb][:], in_=v_scr[128 * h:128 * (h + 1), :]))
        sch.dma("sync", "sa" if hb == 0 else "sb", lambda E, hb=hb, h=h: E.dma_start(
            out=Qsb[hb][:], in_=q_scr[128 * h:128 * (h + 1), :]))
        hd_tok[h] = sch.dma("sync", "sa" if hb == 0 else "sb",
                            lambda E, hb=hb, h=h: E.dma_start(
            out=qbh[hb][:], in_=qbias[h:h + 1, :]))

    for _ in range(16):
        sch.op("scalar", lambda E: E.activation(jnk[:, 0:128], mtri_sb[:], AF.Exp),
               inc=False)
    emit_head_dmas(0)
    for h in range(NH):
        hb = h % 2
        if h + 1 < NH:
            emit_head_dmas(h + 1)
        t_vt = None
        sch.wait("tensor", hd_tok[h])
        for vb in range(4):
            sch.wait("tensor", pT_ev)
            for m in range(8):
                kt = 8 * vb + m
                sch.op("tensor", lambda E, hb=hb, m=m, kt=kt: E.transpose(
                    pT[:, 128 * m:128 * (m + 1)],
                    Vsb[hb][:, 128 * kt:128 * (kt + 1)], id_sb[:]), inc=(m == 7))
            t_vt = ("t", sch.n["t"])
            sch.wait("vector", t_vt)
            if vb == 0:
                sch.wait("vector", Vtok_free[hb])
            sch.op("vector", lambda E, hb=hb, vb=vb: E.tensor_copy(
                Vtok[hb][:, 1024 * vb:1024 * (vb + 1)], pT[:, :]))
            pT_ev = ("v", sch.n["v"])
        Vsb_free[hb] = t_vt
        t_avden = None
        v_attn = None
        sbp = 0

        def finish_gi(gi):
            nonlocal den_free, rden_free, repl_free, av_free, v_attn
            sch.wait("vector", t_avden)
            sch.wait("vector", rden_free)
            sch.op("vector", lambda E: E.reciprocal(rden[:], pE[0:1, 0:512]))
            v_rec = ("v", sch.n["v"])
            den_free = v_rec
            sch.wait("tensor", v_rec)
            sch.wait("tensor", repl_free)
            sch.op("tensor", lambda E: E.matmul(pE[:, 512:1024], ones1f_sb[:],
                                                rden[:], start=True, stop=True))
            t_repl = ("t", sch.n["t"])
            rden_free = t_repl
            sch.wait("vector", t_repl)
            sch.op("vector", lambda E: E.tensor_copy(rrep[:], pE[:, 512:1024]))
            repl_free = ("v", sch.n["v"])
            if gi == 0:
                sch.wait("vector", attn_free[hb])
            sch.op("vector", lambda E, hb=hb, gi=gi: E.tensor_tensor(
                attn[hb][:, 512 * gi:512 * (gi + 1)], pC[:, 0:512], rrep[:],
                ALU.mult))
            v_attn = ("v", sch.n["v"])
            av_free = v_attn

        def emit_scores(gi, kt):
            nonlocal sbp, ptc
            sub = kt if kt < 16 else kt - 16
            qa = max(4 * gi, sub)
            qoff = 128 * (qa - 4 * gi)
            base = pFa if sbp == 0 else pFb
            reg = base[:, qoff:512]
            my_sbp = sbp
            sch.wait("tensor", exp_read[sbp])
            sch.op("tensor", lambda E, hb=hb, gi=gi, qoff=qoff, reg=reg: E.matmul(
                reg, ones1_sb[:],
                qbh[hb][0:1, 512 * gi + qoff:512 * (gi + 1)],
                start=True, stop=False), inc=False)
            sch.op("tensor", lambda E, hb=hb, kt=kt, gi=gi, qoff=qoff, reg=reg: E.matmul(
                reg, Ksb[hb][:, 128 * kt:128 * (kt + 1)],
                Qsb[hb][:, 512 * gi + qoff:512 * (gi + 1)],
                start=False, stop=True))
            t_sc = ("t", sch.n["t"])
            masked = 4 * gi <= sub <= 4 * gi + 3
            v_mask = None
            if masked:
                msb = mtri_sb if kt < 16 else moth_sb
                off = 128 * (sub - 4 * gi)
                mreg = base[:, off:off + 128]
                sch.wait("vector", t_sc)
                sch.op("vector", lambda E, mreg=mreg, msb=msb: E.tensor_tensor(
                    mreg, mreg, msb[:], ALU.add))
                v_mask = ("v", sch.n["v"])
            pb = ptc % 2
            ptc += 1
            sch.wait("scalar", t_sc)
            if masked:
                sch.wait("scalar", v_mask)
            sch.wait("scalar", pt_free[pb])
            sch.op("scalar", lambda E, pb=pb, qoff=qoff, reg=reg, h=h, kt=kt: E.activation(
                ptb_[pb][:, qoff:512], reg, AF.Exp,
                bias=expb_sb[:, 32 * h + kt:32 * h + kt + 1]))
            c_exp = ("c", sch.n["c"])
            exp_read[my_sbp] = c_exp
            sbp ^= 1
            return (kt, pb, qoff, c_exp)

        def emit_avden(st, idx, nk):
            nonlocal t_avden
            kt, pb, qoff, c_exp = st
            sch.wait("tensor", c_exp)
            if idx == 0:
                sch.wait("tensor", av_free)
                sch.wait("tensor", den_free)
            sch.op("tensor", lambda E, hb=hb, kt=kt, pb=pb, qoff=qoff, idx=idx, nk=nk: E.matmul(
                pC[:, qoff:512], Vtok[hb][:, 128 * kt:128 * (kt + 1)],
                ptb_[pb][:, qoff:512], start=(idx == 0), stop=(idx == nk - 1)),
                inc=False)
            sch.op("tensor", lambda E, pb=pb, qoff=qoff, idx=idx, nk=nk: E.matmul(
                pE[0:1, qoff:512], onesc_sb[:], ptb_[pb][:, qoff:512],
                start=(idx == 0), stop=(idx == nk - 1)))
            t_avden = ("t", sch.n["t"])
            pt_free[pb] = t_avden

        all_kts = []
        for gi in range(4):
            kts = list(range(0, 4 * gi + 4)) + list(range(16, 16 + 4 * gi + 4))
            for idx, kt in enumerate(kts):
                all_kts.append((gi, kt, idx, len(kts)))
        staged = None
        for step, (gi, kt, idx, nk) in enumerate(all_kts):
            st_new = emit_scores(gi, kt)
            if staged is not None:
                pgi, pidx, pnk, pst = staged
                emit_avden(pst, pidx, pnk)
                if pidx == pnk - 1:
                    finish_gi(pgi)
            staged = (gi, idx, nk, st_new)
        pgi, pidx, pnk, pst = staged
        emit_avden(pst, pidx, pnk)
        finish_gi(pgi)
        KQ_free[hb] = t_avden
        Vtok_free[hb] = t_avden
        sch.wait("gpsimd", v_attn)
        sch.dma("gpsimd", "gd", lambda E, hb=hb, h=h: E.dma_start(
            out=att_scr[128 * h:128 * (h + 1), :], in_=attn[hb][:, 0:2048]))
        attn_free[hb] = ("gd", sch.n["gd"])
    att_all = ("gd", sch.n["gd"])
    sch.barrier()
    st2.close()

    st3a = ExitStack()
    e3 = st3a.enter_context
    w32 = [e3(nc.sbuf_tensor(f"w32_{i}", [P, D], F32)) for i in range(3)]
    uq = e3(nc.sbuf_tensor("uq", [P, D], BF16))
    tq = [e3(nc.sbuf_tensor(f"tq{i}", [P, D], BF16)) for i in range(2)]
    wstage = [e3(nc.sbuf_tensor(f"wstage{i}", [P, D], BF16)) for i in range(2)]
    w2_r = w2.rearrange("(db p) h -> p db h", p=128)

    jobs = [("w1", j) for j in range(NHT)] + [("w3", j) for j in range(NHT)] \
        + [("w2", j) for j in range(NHT)] + [("wp", j) for j in range(16)]
    scr_of = {"w1": w1t_scr, "w3": w3t_scr, "w2": w2t_scr}
    thr_of = {"w1": 0, "w3": 1, "w2": 2}
    w32_free = [None, None, None]
    tq_free = [None, None]
    wst_free = [None, None]
    pT_ev3 = pT_ev

    w_tok = [None] * len(jobs)

    def emit_w_dma(k):
        kind, j = jobs[k]
        bw = k % 3
        skey = ("sa", "sb", "sc")[bw]
        pad = kind in ("w1", "w3", "w2") and j == NHT - 1
        nrows = HID - 128 * j if pad else 128
        if pad:
            sch.wait("vector", w32_free[bw])
            sch.op("vector", lambda E, bw=bw: E.memset(w32[bw][:], 0.0))
            w32_free[bw] = ("v", sch.n["v"])
        sch.wait("sync", w32_free[bw])
        if kind == "w2":
            w32_3 = w32[bw].ap().rearrange("p (db c) -> p db c", db=16)
            w_tok[k] = sch.dma("sync", skey, lambda E, j=j, nrows=nrows, w32_3=w32_3: E.dma_start(
                out=w32_3[:, :, 0:nrows], in_=w2_r[:, :, 128 * j:128 * j + nrows]))
        elif kind == "wp":
            w_tok[k] = sch.dma("sync", skey, lambda E, bw=bw, j=j: E.dma_start(
                out=w32[bw][:], in_=wproj[128 * j:128 * (j + 1), :]))
        else:
            wt = w1 if kind == "w1" else w3
            w_tok[k] = sch.dma("sync", skey, lambda E, bw=bw, j=j, nrows=nrows, wt=wt: E.dma_start(
                out=w32[bw][0:nrows, :], in_=wt[128 * j:128 * j + nrows, :]))

    for _ in range(16):
        sch.op("scalar", lambda E: E.activation(jnk[:, 0:128], mtri_sb[:], AF.Silu),
               inc=False)
    emit_w_dma(0)
    emit_w_dma(1)
    wscr_done = {}
    for k, (kind, j) in enumerate(jobs):
        b = k % 2
        bw = k % 3
        if k + 2 < len(jobs):
            emit_w_dma(k + 2)
        if kind == "wp":
            sch.wait("vector", w_tok[k])
            sch.wait("vector", wst_free[b])
            sch.op("vector", lambda E, b=b, bw=bw: E.tensor_copy(wstage[b][:], w32[bw][:]))
            w32_free[bw] = ("v", sch.n["v"])
            sch.wait("gpsimd", w32_free[bw])
            sch.dma("gpsimd", "gd", lambda E, b=b, j=j: E.dma_start(
                out=wp_scr[128 * j:128 * (j + 1), :], in_=wstage[b][:]))
            wst_free[b] = ("gd", sch.n["gd"])
            continue
        tc_ = thr_of[kind]
        sch.wait("vector", w_tok[k])
        sch.wait("vector", tok_sdone)
        sch.op("vector", lambda E, bw=bw, tc_=tc_: E.tensor_single_scalar(
            uq[:], w32[bw][:], srepN[:, tc_:tc_ + 1], ALU.is_le), inc=False)
        sch.wait("vector", tq_free[b])
        sch.op("vector", lambda E, b=b, bw=bw, tc_=tc_: E.scalar_tensor_tensor(
            tq[b][:], w32[bw][:], srepP[:, tc_:tc_ + 1], uq[:],
            ALU.is_ge, ALU.subtract))
        w32_free[bw] = ("v", sch.n["v"])
        v_q = ("v", sch.n["v"])
        t_tr = None
        for half in range(2):
            sch.wait("tensor", v_q)
            sch.wait("tensor", pT_ev3)
            for m in range(8):
                db = 8 * half + m
                sch.op("tensor", lambda E, b=b, m=m, db=db: E.transpose(
                    pT[:, 128 * m:128 * (m + 1)],
                    tq[b][:, 128 * db:128 * (db + 1)], id_sb[:]), inc=(m == 7))
            t_tr = ("t", sch.n["t"])
            sch.wait("vector", t_tr)
            if half == 0:
                sch.wait("vector", wst_free[b])
            sch.op("vector", lambda E, b=b, half=half: E.tensor_copy(
                wstage[b][:, 1024 * half:1024 * (half + 1)], pT[:, :]))
            pT_ev3 = ("v", sch.n["v"])
        tq_free[b] = t_tr
        sch.wait("gpsimd", pT_ev3)
        sch.dma("gpsimd", "gd", lambda E, b=b, j=j, scr=scr_of[kind]: E.dma_start(
            out=scr[128 * j:128 * (j + 1), :], in_=wstage[b][:]))
        wst_free[b] = ("gd", sch.n["gd"])
        if j == NHT - 1:
            wscr_done[kind] = ("gd", sch.n["gd"])
    wscr_done["wp"] = ("gd", sch.n["gd"])
    sch.barrier()
    st3a.close()

    psF_ctx.__exit__(None, None, None)
    psT_ctx.__exit__(None, None, None)
    psG_ctx = nc.psum_tensor("psG", [P, 2048], F32)
    pG = psG_ctx.__enter__()

    st3b = ExitStack()
    e3b = st3b.enter_context
    g_sb = e3b(nc.sbuf_tensor("g_sb", [P, NHT * TCH], BF16))
    xnch = e3b(nc.sbuf_tensor("xnch", [P, 16 * TCH], BF16))
    attch = e3b(nc.sbuf_tensor("attch", [P, 16 * TCH], BF16))
    w1qT = [e3b(nc.sbuf_tensor(f"w1qT{i}", [P, D], BF16)) for i in range(2)]
    w3qT = [e3b(nc.sbuf_tensor(f"w3qT{i}", [P, D], BF16)) for i in range(2)]
    w2qT = [e3b(nc.sbuf_tensor(f"w2qT{i}", [P, D], BF16)) for i in range(3)]
    wpt = [e3b(nc.sbuf_tensor(f"wpt{i}", [P, D], BF16)) for i in range(2)]
    xtile = e3b(nc.sbuf_tensor("xtile", [P, D], F32))
    outst = e3b(nc.sbuf_tensor("outst", [P, D], F32))
    silu_t = e3b(nc.sbuf_tensor("silu_t", [P, TCH], F32))

    w1qT_free = [None, None]
    w3qT_free = [None, None]
    w2qT_free = [None, None, None]
    wpt_free = [None, None]
    xtile_free = None
    outst_free = None
    xnch_free = None
    attch_free = None
    g_free = None
    pG_free = [None, None]
    silu_free = None
    pE_out_free = repl_free
    pG_out_free = None

    f1_tok = [None] * NHT
    ch_tok = [None] * NCH

    def emit_f1_dmas(j):
        b = j % 2
        key = "sa" if b == 0 else "sb"
        sch.wait("sync", w1qT_free[b])
        sch.dma("sync", key, lambda E, b=b, j=j: E.dma_start(
            out=w1qT[b][:], in_=w1t_scr[128 * j:128 * (j + 1), :]))
        sch.wait("sync", w3qT_free[b])
        f1_tok[j] = sch.dma("sync", key, lambda E, b=b, j=j: E.dma_start(
            out=w3qT[b][:], in_=w3t_scr[128 * j:128 * (j + 1), :]))

    def emit_chunk_dmas(c):
        sch.wait("sync", xnch_free)
        for db in range(16):
            sch.dma("sync", "sd", lambda E, db=db, c=c: E.dma_start(
                out=xnch[:, TCH * db:TCH * (db + 1)],
                in_=xn_scr[128 * db:128 * (db + 1), TCH * c:TCH * (c + 1)]))
        sch.wait("sync", attch_free)
        for hc in range(16):
            sch.dma("sync", "sd", lambda E, hc=hc, c=c: E.dma_start(
                out=attch[:, TCH * hc:TCH * (hc + 1)],
                in_=att_scr[128 * hc:128 * (hc + 1), TCH * c:TCH * (c + 1)]))
        ch_tok[c] = ("sd", sch.n["sd"])

    emit_chunk_dmas(0)
    emit_f1_dmas(0)
    for c in range(NCH):
        for j in range(NHT):
            b = j % 2
            base = 1024 * (j % 2)
            if j + 1 < NHT:
                emit_f1_dmas(j + 1)
            sch.wait("tensor", f1_tok[j])
            sch.wait("tensor", ch_tok[c])
            if j == 0:
                sch.wait("tensor", g_free)
                sch.wait("tensor", pG_out_free)
            sch.wait("tensor", pG_free[j % 2])
            for db in range(16):
                sch.op("tensor", lambda E, b=b, db=db, base=base: E.matmul(
                    pG[:, base:base + 512], w1qT[b][:, 128 * db:128 * (db + 1)],
                    xnch[:, TCH * db:TCH * (db + 1)],
                    start=(db == 0), stop=(db == 15)), inc=False)
                sch.op("tensor", lambda E, b=b, db=db, base=base: E.matmul(
                    pG[:, base + 512:base + 1024], w3qT[b][:, 128 * db:128 * (db + 1)],
                    xnch[:, TCH * db:TCH * (db + 1)],
                    start=(db == 0), stop=(db == 15)), inc=(db == 15))
            t_h3 = ("t", sch.n["t"])
            t_h1 = t_h3
            w1qT_free[b] = t_h3
            w3qT_free[b] = t_h3
            sch.wait("scalar", t_h1)
            sch.wait("scalar", silu_free)
            sch.op("scalar", lambda E, base=base: E.activation(
                silu_t[:], pG[:, base:base + 512], AF.Silu, scale=srep[:, 0:1]))
            c_silu = ("c", sch.n["c"])
            sch.wait("vector", c_silu)
            if j == 0:
                sch.wait("vector", g_free)
            sch.op("vector", lambda E, j=j, base=base: E.scalar_tensor_tensor(
                g_sb[:, TCH * j:TCH * (j + 1)], silu_t[:], srepM[:, 0:1],
                pG[:, base + 512:base + 1024], ALU.mult, ALU.mult))
            pG_free[j % 2] = ("v", sch.n["v"])
            silu_free = pG_free[j % 2]
        v_g_all = ("v", sch.n["v"])
        xnch_free = ("t", sch.n["t"])
        for pair in range(2):
            tt = (2 * pair, 2 * pair + 1)
            ps_of = {tt[0]: pE, tt[1]: pG}
            for t in tt:
                ps = ps_of[t]
                if ps is pE:
                    sch.wait("tensor", pE_out_free)
                else:
                    sch.wait("tensor", v_g_all)
                    sch.wait("tensor", pG_free[0])
                    sch.wait("tensor", pG_free[1])
                    sch.wait("tensor", pG_out_free)
                for d in range(4):
                    sch.op("tensor", lambda E, ps=ps, d=d: E.matmul(
                        ps[:, 512 * d:512 * (d + 1)], ones1f_sb[:],
                        bproj_sb[0:1, 512 * d:512 * (d + 1)],
                        start=True, stop=False), inc=False)
            sch.wait("sync", wpt_free[0])
            if c == 0 and pair == 0:
                sch.wait("sync", wscr_done["wp"])
            wp_tok = [None] * 16
            wp_tok[0] = sch.dma("sync", "sa", lambda E: E.dma_start(
                out=wpt[0][:], in_=wp_scr[0:128, :]))
            for hc in range(16):
                bw = hc % 2
                if hc + 1 < 16:
                    bw2 = (hc + 1) % 2
                    sch.wait("sync", wpt_free[bw2])
                    wp_tok[hc + 1] = sch.dma("sync", "sa" if bw2 == 0 else "sb",
                                             lambda E, bw2=bw2, hc=hc: E.dma_start(
                        out=wpt[bw2][:], in_=wp_scr[128 * (hc + 1):128 * (hc + 2), :]))
                sch.wait("tensor", wp_tok[hc])
                if hc == 0:
                    sch.wait("tensor", ch_tok[c])
                for t in tt:
                    ps = ps_of[t]
                    for d in range(4):
                        sch.op("tensor", lambda E, ps=ps, hc=hc, d=d, t=t, bw=bw: E.matmul(
                            ps[:, 512 * d:512 * (d + 1)],
                            attch[:, TCH * hc + 128 * t:TCH * hc + 128 * (t + 1)],
                            wpt[bw][:, 512 * d:512 * (d + 1)],
                            start=False, stop=False), inc=(t == tt[1] and d == 3))
                wpt_free[bw] = ("t", sch.n["t"])
            if c == 0 and pair == 0:
                sch.wait("sync", wscr_done["w2"])
            w2_tok = [None] * NHT

            def emit_w2(j):
                s3 = j % 3
                sch.wait("sync", w2qT_free[s3])
                w2_tok[j] = sch.dma("sync", ("sa", "sb", "sc")[s3],
                                    lambda E, s3=s3, j=j: E.dma_start(
                    out=w2qT[s3][:], in_=w2t_scr[128 * j:128 * (j + 1), :]))

            emit_w2(0)
            emit_w2(1)
            for j in range(NHT):
                b3 = j % 3
                if j + 2 < NHT:
                    emit_w2(j + 2)
                sch.wait("tensor", w2_tok[j])
                if j == 0:
                    sch.wait("tensor", v_g_all)
                for t in tt:
                    ps = ps_of[t]
                    for d in range(4):
                        sch.op("tensor", lambda E, ps=ps, j=j, d=d, t=t, b3=b3: E.matmul(
                            ps[:, 512 * d:512 * (d + 1)],
                            g_sb[:, TCH * j + 128 * t:TCH * j + 128 * (t + 1)],
                            w2qT[b3][:, 512 * d:512 * (d + 1)],
                            start=False, stop=(j == NHT - 1)),
                            inc=(t == tt[1] and d == 3))
                w2qT_free[b3] = ("t", sch.n["t"])
            t_pair = ("t", sch.n["t"])
            g_free = t_pair
            for t in tt:
                ps = ps_of[t]
                sch.wait("sync", xtile_free)
                xt_tok2 = sch.dma("sync", "sd", lambda E, c=c, t=t: E.dma_start(
                    out=xtile[:], in_=x_ap[TCH * c + 128 * t:TCH * c + 128 * (t + 1), :]))
                sch.wait("vector", t_pair)
                sch.wait("vector", xt_tok2)
                sch.wait("vector", outst_free)
                sch.op("vector", lambda E, ps=ps: E.tensor_tensor(
                    outst[:], ps[:, 0:2048], xtile[:], ALU.add))
                v_ev = ("v", sch.n["v"])
                xtile_free = v_ev
                if ps is pE:
                    pE_out_free = v_ev
                else:
                    pG_out_free = v_ev
                sch.wait("gpsimd", v_ev)
                sch.dma("gpsimd", "gd", lambda E, c=c, t=t: E.dma_start(
                    out=out_ap[TCH * c + 128 * t:TCH * c + 128 * (t + 1), :],
                    in_=outst[:]))
                outst_free = ("gd", sch.n["gd"])
        attch_free = ("t", sch.n["t"])
        if c + 1 < NCH:
            emit_chunk_dmas(c + 1)
            emit_f1_dmas(0)
    final_gd = sch.n["gd"]
    st3b.close()
    psG_ctx.__exit__(None, None, None)
    psE_ctx.__exit__(None, None, None)

    sch.q["gpsimd"].append(lambda E, v=final_gd * 16: E.wait_ge(sch.sems["gd"], v))

    with nc.Block() as blk:
        blk.sync(lambda E: [f(E) for f in sch.q["sync"]])
        blk.tensor(lambda E: [f(E) for f in sch.q["tensor"]])
        blk.vector(lambda E: [f(E) for f in sch.q["vector"]])
        blk.scalar(lambda E: [f(E) for f in sch.q["scalar"]])
        blk.gpsimd(lambda E: [f(E) for f in sch.q["gpsimd"]])

    big.close()
    nc._sch_counts = dict(sch.n)
    return nc


_CACHE = {}


def _alibi_slopes(n):
    def pow2_slopes(n):
        start = 2.0 ** (-2.0 ** (-(math.log2(n) - 3)))
        return [start * start ** i for i in range(n)]
    if math.log2(n).is_integer():
        return pow2_slopes(n)
    cp2 = 2 ** math.floor(math.log2(n))
    return pow2_slopes(cp2) + _alibi_slopes(2 * cp2)[0::2][:n - cp2]


def _consts(p):
    slopes = np.asarray(_alibi_slopes(NH), np.float64)
    gk = np.zeros(S, np.int64)
    for j in range(32):
        gt = 2 * j + p if j < 16 else 2 * (j - 16) + (1 - p)
        gk[128 * j:128 * (j + 1)] = np.arange(128 * gt, 128 * gt + 128)
    expb = np.zeros((P, 512), np.float32)
    for h in range(NH):
        for kt in range(32):
            expb[:, 32 * h + kt] = slopes[h] * gk[128 * kt:128 * (kt + 1)]
    qb = np.zeros((NH, D), np.float64)
    for h in range(NH):
        qb[h, :] = -slopes[h] * gk[0:2048]
    qb = qb.astype(ml_dtypes.bfloat16)
    r = np.arange(P)
    mtri = np.where(r[:, None] > r[None, :], MASKNEG, 0.0).astype(np.float32)
    moth = (np.full((P, P), MASKNEG, np.float32) if p == 0
            else np.zeros((P, P), np.float32))
    return expb, qb, mtri, moth


def kernel(L0_attn_hidden_states, L0_attn_w_qkv, L0_attn_b_qkv, L0_attn_w_proj,
           L0_attn_b_proj, L0_ffn_w1, L0_ffn_w3, L0_ffn_w2):
    x = np.ascontiguousarray(np.asarray(L0_attn_hidden_states, np.float32))
    wqkv = np.ascontiguousarray(np.asarray(L0_attn_w_qkv, np.float32))
    bqkv = np.asarray(L0_attn_b_qkv, np.float32)
    wproj = np.ascontiguousarray(np.asarray(L0_attn_w_proj, np.float32))
    bpro = np.asarray(L0_attn_b_proj, np.float32)
    w1 = np.ascontiguousarray(np.asarray(L0_ffn_w1, np.float32))
    w3 = np.ascontiguousarray(np.asarray(L0_ffn_w3, np.float32))
    w2 = np.ascontiguousarray(np.asarray(L0_ffn_w2, np.float32))

    if "nc" not in _CACHE:
        _CACHE["nc"] = build_nc()
    nc = _CACHE["nc"]

    bqkvt = np.ascontiguousarray(bqkv.reshape(48, 128).T)
    bproj2 = np.ascontiguousarray(bpro.reshape(1, D))
    ident = np.eye(P).astype(ml_dtypes.bfloat16)
    ones1 = np.ones((1, P), ml_dtypes.bfloat16)
    ones1f = np.ones((1, P), np.float32)
    onesc = np.ones((P, 1), ml_dtypes.bfloat16)
    onescf = np.ones((P, 1), np.float32)

    perms = {}
    in_maps = []
    for core in range(8):
        b, p = core // 2, core % 2
        if p not in perms:
            own = np.concatenate(
                [np.arange(128 * (2 * i + p), 128 * (2 * i + p) + 128)
                 for i in range(16)])
            oth = np.concatenate(
                [np.arange(128 * (2 * i + 1 - p), 128 * (2 * i + 1 - p) + 128)
                 for i in range(16)])
            expb, qb, mtri, moth = _consts(p)
            perms[p] = (np.concatenate([own, oth]), own, expb, qb, mtri, moth)
        perm, own, expb, qb, mtri, moth = perms[p]
        xp = np.ascontiguousarray(x[b][perm])
        in_maps.append({
            "x": xp, "wqkv": wqkv, "bqkvt": bqkvt, "wproj": wproj,
            "bproj": bproj2, "w1": w1, "w3": w3, "w2": w2,
            "expb": expb, "qbias": qb, "mtri": mtri, "moth": moth,
            "ident": ident, "ones1": ones1, "ones1f": ones1f,
            "onesc": onesc, "onescf": onescf,
        })

    res = run_bass_kernel_spmd(nc, in_maps, core_ids=list(range(8)))

    out = np.empty((4, S, D), np.float32)
    for core in range(8):
        b, p = core // 2, core % 2
        out[b][perms[p][1]] = res.results[core]["out"]
    return out
